# version 16
# baseline (speedup 1.0000x reference)
import contextlib
import math

import numpy as np
import ml_dtypes

import concourse.bass as bass
import concourse.mybir as mybir
from concourse.bass_utils import run_bass_kernel_spmd

F32 = mybir.dt.float32
BF16 = mybir.dt.bfloat16
AF = mybir.ActivationFunctionType
ALU = mybir.AluOpType
ET = mybir.EngineType

S = 8192
D = 4096
HD = 128
NKT = D // 128
NCH = S // 512
OWN = S // 8
QH = 4
QKVO = 768
EPS = 1e-5
SM_SCALE = 1.0 / math.sqrt(128.0)
SM_SHIFT = -20.0
f = np.float32

TQ, VQ, SQ, GQ, YQ = "t", "v", "s", "g", "y"


class Sem:
    __slots__ = ("h", "n", "name")

    def __init__(self, h, name):
        self.h = h
        self.n = 0
        self.name = name


class Prog:

    def __init__(self):
        self.q = {k: [] for k in (TQ, VQ, SQ, GQ, YQ)}
        self.hw = {k: {} for k in (TQ, VQ, SQ, GQ, YQ)}

    def emit(self, qn, fn):
        self.q[qn].append(fn)

    def wait(self, qn, sem, val):
        if val <= 0:
            return
        if val <= self.hw[qn].get(sem.name, 0):
            return
        self.hw[qn][sem.name] = val
        self.q[qn].append(lambda e, h=sem.h, v=val: e.wait_ge(h, v))

    def flush(self, block):
        attr = {TQ: "tensor", VQ: "vector", SQ: "scalar", GQ: "gpsimd", YQ: "sync"}
        for qn, name in attr.items():
            ops = self.q[qn]
            if not ops:
                continue
            self.q[qn] = []

            def body(e, ops=ops):
                for op in ops:
                    op(e)

            getattr(block, name)(body)


def build_nc(phases=4, dbg=False, p2_units=64, p2_ag=True, p2_ffn=True):
    nc = bass.Bass("TRN2", target_bir_lowering=False)

    xT = nc.dram_tensor("xT", [D, S], F32, kind="ExternalInput").ap()
    x_own = nc.dram_tensor("x_own", [D, OWN], F32, kind="ExternalInput").ap()
    wqkvT = nc.dram_tensor("wqkvT", [D, QKVO], F32, kind="ExternalInput").ap()
    bqkv = nc.dram_tensor("bqkv", [1, QKVO], F32, kind="ExternalInput").ap()
    woT = nc.dram_tensor("woT", [D, D], F32, kind="ExternalInput").ap()
    WT = nc.dram_tensor("WT", [D, D], F32, kind="ExternalInput").ap()
    AcatT = nc.dram_tensor("AcatT", [D, 128], F32, kind="ExternalInput").ap()
    Bcat = nc.dram_tensor("Bcat", [128, D], F32, kind="ExternalInput").ap()
    b_o = nc.dram_tensor("b_o", [1, D], F32, kind="ExternalInput").ap()
    fbias = nc.dram_tensor("fbias", [1, D], F32, kind="ExternalInput").ap()
    lwr = nc.dram_tensor("lwr", [128, 1], F32, kind="ExternalInput").ap()
    cosT = nc.dram_tensor("cosT", [128, S], F32, kind="ExternalInput").ap()
    sinT = nc.dram_tensor("sinT", [128, S], F32, kind="ExternalInput").ap()
    swapm = nc.dram_tensor("swapm", [128, 128], BF16, kind="ExternalInput").ap()
    masks = nc.dram_tensor("masks", [128, 4 * 512], BF16, kind="ExternalInput").ap()
    ident = nc.dram_tensor("ident", [128, 128], BF16, kind="ExternalInput").ap()

    q_scr = nc.dram_tensor("q_scr", [512, S], BF16)
    g_scr = nc.dram_tensor("g_scr", [D, OWN], BF16)
    ag_in = nc.dram_tensor("ag_in", [512, S], BF16)
    ag_out = nc.dram_tensor("ag_out", [D, S], BF16, addr_space="Shared")
    st_in = nc.dram_tensor("st_in", [3, OWN], F32)
    st_out = nc.dram_tensor("st_out", [24, OWN], F32)

    outT = nc.dram_tensor("outT", [D, OWN], F32, kind="ExternalOutput").ap()
    if dbg:
        dbg_st = nc.dram_tensor("dbg_st", [24, OWN], F32, kind="ExternalOutput")
        dbg_aug2 = nc.dram_tensor("dbg_aug2", [2, S], BF16, kind="ExternalOutput")
        dbg_aug2o = nc.dram_tensor("dbg_aug2o", [2, OWN], BF16, kind="ExternalOutput")
        dbg_rb = nc.dram_tensor("dbg_rb", [128, OWN], F32, kind="ExternalOutput")
        dbg_q = nc.dram_tensor("dbg_q", [512, S], BF16, kind="ExternalOutput")
        dbg_kT = nc.dram_tensor("dbg_kT", [128, S], BF16, kind="ExternalOutput")
        dbg_va = nc.dram_tensor("dbg_va", [128, 64 * 130], BF16, kind="ExternalOutput")
        dbg_agin = nc.dram_tensor("dbg_agin", [512, S], BF16, kind="ExternalOutput")
        dbg_agout = nc.dram_tensor("dbg_agout", [D, S], BF16, kind="ExternalOutput")
        dbg_g = nc.dram_tensor("dbg_g", [D, OWN], BF16, kind="ExternalOutput")
        dbg_hs = nc.dram_tensor("dbg_hs", [128, OWN], BF16, kind="ExternalOutput")

    P = Prog()
    st = contextlib.ExitStack()

    def sem(name):
        return Sem(st.enter_context(nc.semaphore(name)), name)

    d_wq, d_x, d_xs, d_cs, d_misc = sem("d_wq"), sem("d_x"), sem("d_xs"), sem("d_cs"), sem("d_misc")
    d_q, d_qb, d_st, d_fb = sem("d_q"), sem("d_qb"), sem("d_st"), sem("d_fb")
    d_wW, d_ac, d_bc, d_att = sem("d_wW"), sem("d_ac"), sem("d_bc"), sem("d_att")
    d_agl, d_wo, d_g, d_gw, d_xr, d_out, d_xo = (
        sem("d_agl"), sem("d_wo"), sem("d_g"), sem("d_gw"), sem("d_xr"), sem("d_out"), sem("d_xo"))
    c_ag1, c_ag2 = sem("c_ag1"), sem("c_ag2")
    k_init, k_sq, k_sumA, k_sumB, k_strow, k_s = (
        sem("k_init"), sem("k_sq"), sem("k_sumA"), sem("k_sumB"), sem("k_strow"), sem("k_s"))
    k_rbo = sem("k_rbo")
    k_wcq, k_wcqev, k_qmm, k_qev, k_rbmm, k_rbev, k_rope, k_trv, k_vev = (
        sem("k_wcq"), sem("k_wcqev"), sem("k_qmm"), sem("k_qev"), sem("k_rbmm"),
        sem("k_rbev"), sem("k_rope"), sem("k_trv"), sem("k_vev"))
    k_sc, k_exp, k_msk, k_avg, k_rc, k_div, k_trm, k_trev = (
        sem("k_sc"), sem("k_exp"), sem("k_msk"), sem("k_avg"), sem("k_rc"),
        sem("k_div"), sem("k_trm"), sem("k_trev"))
    k_wcA, k_haev, k_hmm, k_hsev, k_wcf, k_wcev, k_ffn, k_gev = (
        sem("k_wcA"), sem("k_haev"), sem("k_hmm"), sem("k_hsev"), sem("k_wcf"),
        sem("k_wcev"), sem("k_ffn"), sem("k_gev"))
    k_op, k_silu, k_v, k_ep = sem("k_op"), sem("k_silu"), sem("k_v"), sem("k_ep")
    k_swp = sem("k_swp")
    d_wb = sem("d_wb")

    def sb(stk, name, shape, dtype):
        return stk.enter_context(nc.sbuf_tensor(name, shape, dtype))

    def pst(stk, name, shape, dtype):
        return stk.enter_context(nc.psum_tensor(name, shape, dtype))

    def mm(out_ap, lhsT, rhs, start, stop, inc=None, inck=1):
        if inc is not None:
            inc.n += inck

        def fn(e, o=out_ap, l=lhsT, r=rhs, a=start, b=stop, i=inc, k=inck):
            ins = e.matmul(o, l, r, start=a, stop=b, skip_group_check=True)
            if i is not None:
                ins.then_inc(i.h, k)
        P.emit(TQ, fn)

    def tr(out_ap, in_ap, inc=None):
        if inc is not None:
            inc.n += 1

        def fn(e, o=out_ap, x=in_ap, i=inc):
            ins = e.transpose(o, x, id_sb[:, :])
            if i is not None:
                ins.then_inc(i.h, 1)
        P.emit(TQ, fn)

    def vop(qn, meth, *args, inc=None, inck=1, **kw):
        if inc is not None:
            inc.n += inck

        def fn(e, m=meth, a=args, i=inc, k=inck, kw=kw):
            ins = getattr(e, m)(*a, **kw)
            if i is not None:
                ins.then_inc(i.h, k)
        P.emit(qn, fn)

    def act(qn, out_ap, in_ap, func, scale, bias, inc=None, accum=None):
        if inc is not None:
            inc.n += 1

        def fn(e, o=out_ap, x=in_ap, fu=func, sc=scale, bi=bias, i=inc, ac=accum):
            kw = {}
            if ac is not None:
                kw["accum_out"] = ac
            ins = e.activation(o, x, fu, bias=bi, scale=sc, **kw)
            if i is not None:
                ins.then_inc(i.h, 1)
        P.emit(qn, fn)

    def dma(qn, out_ap, in_ap, inc):
        inc.n += 16

        def fn(e, o=out_ap, x=in_ap, i=inc):
            e.dma_start(out=o, in_=x).then_inc(i.h, 16)
        P.emit(qn, fn)

    def tiled(dram_ap, cols, kt0, kt1, width):
        return dram_ap[kt0 * 128:kt1 * 128, cols].rearrange("(kt p) n -> p kt n", p=128)

    if True:
        kT = sb(st, "kT", [128, S], BF16)
        vaug = sb(st, "vaug", [128, 64 * 130], BF16)
        aug2o = sb(st, "aug2o", [2, OWN], BF16)
        rb_own = sb(st, "rb_own", [128, OWN], F32)
        id_sb = sb(st, "id_sb", [128, 128], BF16)
        swapm_sb = sb(st, "swapm_sb", [128, 128], BF16)
        ones_col_f = sb(st, "ones_col_f", [128, 1], F32)
        ones_row_f = sb(st, "ones_row_f", [1, 128], F32)
        ones_col_b = sb(st, "ones_col_b", [128, 1], BF16)
        ones_row_b = sb(st, "ones_row_b", [1, 512], BF16)
        ha_auglhs = sb(st, "ha_auglhs", [2, 128], BF16)
        ffn_auglhs = sb(st, "ffn_auglhs", [2, D], BF16)
        hsT = sb(st, "hsT", [128, OWN], BF16)
        block = st.enter_context(nc.Block())
        vop(VQ, "memset", ones_col_f[:, :], 1.0)
        vop(VQ, "memset", ones_row_f[:, :], 1.0)
        vop(VQ, "memset", ones_col_b[:, :], 1.0)
        vop(VQ, "memset", ones_row_b[:, :], 1.0)
        vop(VQ, "memset", ha_auglhs[:, :], 0.0)
        vaug_t = vaug[:, :].rearrange("p (t c) -> p t c", c=130)
        vop(VQ, "memset", vaug_t[:, :, 128:129], 1.0, inc=k_init)

        dma(YQ, id_sb[:, :], ident, d_misc)
        dma(YQ, swapm_sb[:, :], swapm, d_misc)

        stkA = st.enter_context(contextlib.ExitStack())
        aug2 = sb(stkA, "aug2", [2, S], BF16)
        qkv_auglhs = sb(stkA, "qkv_auglhs", [2, QKVO], BF16)
        r_chunk = sb(stkA, "r_chunk", [1, 2 * 512], F32)
        with contextlib.ExitStack() as stk:
            xf32 = sb(stk, "xf32", [128, 2 * OWN], F32)
            xsq = sb(stk, "xsq", [128, 2 * OWN], F32)
            negm_l = sb(stk, "negm_l", [1, OWN], F32)
            invr_l = sb(stk, "invr_l", [1, OWN], F32)
            r_l = sb(stk, "r_l", [1, OWN], F32)
            lnrow = sb(stk, "lnrow", [1, OWN], F32)
            ex2 = sb(stk, "ex2", [1, OWN], F32)
            m2 = sb(stk, "m2", [1, OWN], F32)
            sx_ps = pst(stk, "sx_ps", [1, OWN], F32)
            sx2_ps = pst(stk, "sx2_ps", [1, OWN], F32)
            rbo_ps = pst(stk, "rbo_ps", [128, OWN], F32)
            for kt in range(NKT):
                par = kt % 2
                xf = xf32[:, par * OWN:(par + 1) * OWN]
                xq = xsq[:, par * OWN:(par + 1) * OWN]
                P.wait(YQ, k_sq, kt - 1)
                P.wait(YQ, k_sumA, kt - 1)
                dma(YQ, xf, x_own[kt * 128:(kt + 1) * 128, :], d_xs)
                P.wait(SQ, d_xs, 16 * (kt + 1))
                P.wait(SQ, k_sumB, kt - 1)
                act(SQ, xq, xf, AF.Square, f(1.0), f(0.0), inc=k_sq)
                P.wait(TQ, k_init, 1)
                P.wait(TQ, d_xs, 16 * (kt + 1))
                mm(sx_ps[0:1, 0:512], ones_col_f[:, :], xf[:, 0:512], kt == 0, kt == NKT - 1)
                mm(sx_ps[0:1, 512:1024], ones_col_f[:, :], xf[:, 512:1024], kt == 0, kt == NKT - 1,
                   inc=k_sumA)
                P.wait(TQ, k_sq, kt + 1)
                mm(sx2_ps[0:1, 0:512], ones_col_f[:, :], xq[:, 0:512], kt == 0, kt == NKT - 1)
                mm(sx2_ps[0:1, 512:1024], ones_col_f[:, :], xq[:, 512:1024], kt == 0, kt == NKT - 1,
                   inc=k_sumB)
            P.wait(VQ, k_sumA, NKT)
            P.wait(VQ, k_sumB, NKT)
            vop(VQ, "tensor_scalar_mul", negm_l[:, :], sx_ps[0:1, :], -1.0 / D)
            vop(VQ, "tensor_scalar_mul", ex2[:, :], sx2_ps[0:1, :], 1.0 / D, inc=k_v)
            P.wait(VQ, k_v, k_v.n)
            vop(VQ, "tensor_mul", m2[:, :], negm_l[:, :], negm_l[:, :], inc=k_v)
            P.wait(VQ, k_v, k_v.n)
            vop(VQ, "tensor_sub", m2[:, :], ex2[:, :], m2[:, :], inc=k_v)
            P.wait(SQ, k_v, k_v.n)
            act(SQ, lnrow[:, :], m2[:, :], AF.Ln, f(1.0), f(EPS), inc=k_s)
            P.wait(SQ, k_s, k_s.n)
            act(SQ, invr_l[:, :], lnrow[:, :], AF.Exp, f(0.5), f(0.0), inc=k_s)
            act(SQ, r_l[:, :], lnrow[:, :], AF.Exp, f(-0.5), f(0.0), inc=k_s)
            P.wait(TQ, k_s, k_s.n)
            mm(rbo_ps[:, 0:512], ones_row_f[:, :], r_l[0:1, 0:512], True, True)
            mm(rbo_ps[:, 512:1024], ones_row_f[:, :], r_l[0:1, 512:1024], True, True, inc=k_rbo)
            P.wait(VQ, k_rbo, 1)
            vop(VQ, "tensor_copy", rb_own[:, :], rbo_ps[:, :], inc=k_strow)
            P.wait(YQ, k_s, k_s.n)
            P.wait(YQ, k_v, k_v.n)
            dma(YQ, st_in.ap()[0:1, :], negm_l[:, :], d_st)
            dma(YQ, st_in.ap()[1:2, :], invr_l[:, :], d_st)
            dma(YQ, st_in.ap()[2:3, :], r_l[:, :], d_st)
            P.wait(GQ, d_st, 48)
            c_ag1.n += 1
            P.emit(GQ, lambda e: e.collective_compute(
                "AllGather", ALU.bypass, replica_groups=[list(range(8))],
                ins=[st_in.ap().opt()], outs=[st_out.ap().opt()]).then_inc(c_ag1.h, 1))
            P.wait(GQ, c_ag1, 1)
            for row in (0, 1):
                dma(GQ, aug2[row:row + 1, :].rearrange("p (rr n) -> p rr n", n=OWN),
                    st_out.ap()[row:24:3, :].rearrange("(o rr) n -> o rr n", o=1), d_st)
            d_st_aug2 = d_st.n
            dma(GQ, aug2o[0:1, :], negm_l[0:1, :], d_st)
            dma(GQ, aug2o[1:2, :], invr_l[0:1, :], d_st)
            d_st_aug2o = d_st.n
            if dbg and phases == 1:
                P.wait(GQ, d_st, d_st_aug2o)
                P.wait(GQ, k_strow, 1)
                dma(GQ, dbg_st.ap()[:, :], st_out.ap()[:, :], d_out)
                dma(GQ, dbg_aug2.ap()[:, :], aug2[:, :], d_out)
                dma(GQ, dbg_aug2o.ap()[:, :], aug2o[:, :], d_out)
                dma(GQ, dbg_rb.ap()[:, :], rb_own[:, :], d_out)
                P.wait(GQ, d_out, d_out.n)
            P.flush(block)

        if phases <= 1:
            st.close()
            return nc

        if True:
            stk = stkA
            wqkv_sb = sb(stk, "wqkv_sb", [128, NKT * QKVO], BF16)
            xchunk = sb(stk, "xchunk", [128, 2 * NKT * 512], BF16)
            csd = sb(stk, "csd", [128, 2 * 1024], F32)
            qkvb = sb(stk, "qkvb", [128, 2 * 512], BF16)
            ta = sb(stk, "ta", [128, 512], F32)
            tb = sb(stk, "tb", [128, 512], F32)
            qstage = sb(stk, "qstage", [128, 2 * 512], BF16)
            vT_stage = sb(stk, "vT_stage", [128, 512], BF16)
            rb_sb = sb(stk, "rb_sb", [128, 2 * 512], F32)
            qkv_ps0 = pst(stk, "qkv_ps0", [128, 512], F32)
            qkv_ps1 = pst(stk, "qkv_ps1", [128, 512], F32)
            rbq_ps = pst(stk, "rbq_ps", [128, 512], F32)
            trv_ps = pst(stk, "trv_ps", [128, 512], BF16)
            wcq_ps = pst(stk, "wcq_ps", [1, QKVO], F32)
            sw_ps = pst(stk, "sw_ps", [128, 2 * 512], F32)
            qkv_ps = (qkv_ps0, qkv_ps1)
            P.wait(GQ, k_sumA, NKT)
            P.wait(GQ, k_sumB, NKT)
            P.wait(GQ, k_sq, NKT)
            for i in range(4):
                dma(GQ, wqkv_sb[:, :].rearrange("p (kt o) -> p kt o", o=QKVO)[:, i * 8:(i + 1) * 8, :],
                    tiled(wqkvT, slice(None), i * 8, (i + 1) * 8, QKVO), d_wq)

            dma(GQ, qkv_auglhs[1:2, :], bqkv, d_fb)
            P.wait(TQ, k_strow, 1)
            P.wait(TQ, d_wq, 64)
            for ot in range(6):
                for kt in range(NKT):
                    mm(wcq_ps[0:1, ot * 128:(ot + 1) * 128], ones_col_b[:, :],
                       wqkv_sb[:, kt * QKVO + ot * 128:kt * QKVO + (ot + 1) * 128],
                       kt == 0, kt == NKT - 1, inc=(k_wcq if kt == NKT - 1 else None))
            P.wait(VQ, k_wcq, 6)
            vop(VQ, "tensor_copy", qkv_auglhs[0:1, :], wcq_ps[0:1, :], inc=k_wcqev)
            P.wait(TQ, d_fb, 16)

            n_qdma = 0
            for ch in range(NCH):
                par = ch % 2
                cols = slice(ch * 512, (ch + 1) * 512)
                xc = xchunk[:, par * NKT * 512:(par + 1) * NKT * 512]
                P.wait(GQ, k_qmm, 6 * (ch - 1))
                for i in range(4):
                    dma(GQ, xc.rearrange("p (kt n) -> p kt n", n=512)[:, i * 8:(i + 1) * 8, :],
                        tiled(xT, cols, i * 8, (i + 1) * 8, 512), d_x)
                P.wait(YQ, k_rope, 5 * (ch - 1))
                dma(YQ, csd[:, par * 1024:par * 1024 + 512], cosT[:, cols], d_cs)
                dma(YQ, csd[:, par * 1024 + 512:par * 1024 + 1024], sinT[:, cols], d_cs)
                rr, rcl = ch // 2, (ch % 2) * 512
                P.wait(YQ, c_ag1, 1)
                P.wait(YQ, k_rbmm, ch - 1)
                dma(YQ, r_chunk[0:1, par * 512:(par + 1) * 512],
                    st_out.ap()[3 * rr + 2:3 * rr + 3, rcl:rcl + 512], d_cs)
                P.wait(TQ, k_rbev, ch)
                P.wait(TQ, d_cs, 48 * (ch + 1))
                mm(rbq_ps[:, :], ones_row_f[:, :], r_chunk[0:1, par * 512:(par + 1) * 512],
                   True, True, inc=k_rbmm)
                P.wait(TQ, d_x, 64 * (ch + 1))
                P.wait(TQ, k_wcqev, 1)
                P.wait(TQ, d_st, d_st_aug2)
                for ot in range(6):
                    g_idx = 6 * ch + ot
                    qp = qkv_ps[g_idx % 2]
                    P.wait(TQ, k_qev, g_idx - 1)
                    for kt in range(NKT):
                        mm(qp[:, :], wqkv_sb[:, kt * QKVO + ot * 128:kt * QKVO + (ot + 1) * 128],
                           xc[:, kt * 512:(kt + 1) * 512], kt == 0, False)
                    mm(qp[:, :], qkv_auglhs[:, ot * 128:(ot + 1) * 128], aug2[:, cols],
                       False, True, inc=k_qmm)
                    P.wait(VQ, k_qmm, g_idx + 1)
                    if ot == 0:
                        P.wait(VQ, k_rbmm, ch + 1)
                        vop(VQ, "tensor_copy", rb_sb[:, par * 512:(par + 1) * 512],
                            rbq_ps[:, :], inc=k_rbev)
                        P.wait(VQ, k_rbev, ch + 1)
                    rbs = rb_sb[:, par * 512:(par + 1) * 512]
                    if ot == 5:
                        P.wait(VQ, k_trv, 4 * ch)
                        vop(VQ, "tensor_mul", vT_stage[:, :], qp[:, :], rbs, inc=k_qev)
                        continue
                    ri = 5 * ch + ot
                    qbs = qkvb[:, (ri % 2) * 512:(ri % 2 + 1) * 512]
                    P.wait(VQ, k_swp, ri - 1)
                    vop(VQ, "tensor_mul", qbs, qp[:, :], rbs, inc=k_qev)
                    P.wait(TQ, k_qev, k_qev.n)
                    P.wait(TQ, k_rope, ri - 1)
                    P.wait(TQ, d_misc, 32)
                    mm(sw_ps[:, (ri % 2) * 512:(ri % 2 + 1) * 512], swapm_sb[:, :], qbs,
                       True, True, inc=k_swp)
                    P.wait(VQ, d_cs, 48 * (ch + 1))
                    coss = csd[:, par * 1024:par * 1024 + 512]
                    sins = csd[:, par * 1024 + 512:par * 1024 + 1024]
                    if ot < 4:
                        dst = qstage[:, (ot % 2) * 512:(ot % 2 + 1) * 512]
                        P.wait(VQ, d_q, 16 * (4 * ch + ot - 1))
                    else:
                        dst = kT[:, cols]
                    vop(VQ, "tensor_mul", ta[:, :], qbs, coss)
                    P.wait(VQ, k_swp, ri + 1)
                    vop(VQ, "tensor_mul", tb[:, :],
                        sw_ps[:, (ri % 2) * 512:(ri % 2 + 1) * 512], sins, inc=k_v)
                    P.wait(VQ, k_v, k_v.n)
                    vop(VQ, "tensor_add", dst, ta[:, :], tb[:, :], inc=k_rope)
                    if ot < 4:
                        P.wait(YQ, k_rope, k_rope.n)
                        dma(YQ, q_scr.ap()[ot * 128:(ot + 1) * 128, cols], dst, d_q)
                        n_qdma += 1
                P.wait(TQ, k_qev, 6 * ch + 6)
                P.wait(TQ, k_vev, ch)
                P.wait(TQ, d_misc, 16)
                for i in range(4):
                    tr(trv_ps[:, i * 128:(i + 1) * 128], vT_stage[:, i * 128:(i + 1) * 128],
                       inc=k_trv)
                P.wait(SQ, k_trv, 4 * (ch + 1))
                act(SQ, vaug_t[:, 4 * ch:4 * ch + 4, 0:128],
                    trv_ps[:, :].rearrange("p (t c) -> p t c", c=128), AF.Identity,
                    f(1.0), f(0.0), inc=k_vev)
            assert k_qmm.n == 96 and k_rope.n == 80 and k_vev.n == 16 and n_qdma == 64
            if dbg and phases == 2:
                P.wait(YQ, d_q, 16 * 64)
                P.wait(YQ, k_vev, 16)
                P.wait(YQ, k_rope, 80)
                P.wait(YQ, k_qmm, 96)
                dma(YQ, dbg_q.ap()[:, :], q_scr.ap()[:, :], d_out)
                dma(YQ, dbg_kT.ap()[:, :], kT[:, :], d_out)
                dma(YQ, dbg_va.ap()[:, :], vaug[:, :], d_out)
                dma(YQ, dbg_aug2.ap()[:, :], aug2[:, :], d_out)
                P.wait(YQ, d_out, d_out.n)
            P.flush(block)
            stkA.close()

        if phases <= 2:
            st.close()
            return nc

        with contextlib.ExitStack() as stk:
            qblk = sb(stk, "qblk", [128, 2 * 512], BF16)
            masks_sb = sb(stk, "masks_sb", [128, 4 * 512], BF16)
            pT = sb(stk, "pT", [128, 2 * 1024], BF16)
            att_nat = sb(stk, "att_nat", [128, 512], BF16)
            rc = sb(stk, "rc", [128, 4], F32)
            attst = sb(stk, "attst", [128, S], BF16)
            x_own_sb = sb(stk, "x_own_sb", [128, NKT * 1024], BF16)
            wW_sb = sb(stk, "wW_sb", [128, 2 * NKT * 128], BF16)
            g_tile = sb(stk, "g_tile", [128, 2 * 512], BF16)
            Acat_sb = sb(stk, "Acat_sb", [128, NKT * 128], BF16)
            Bcat_sb = sb(stk, "Bcat_sb", [128, D], BF16)
            lwr_sb = sb(stk, "lwr_sb", [128, 1], F32)
            sc_ps0 = pst(stk, "sc_ps0", [128, 1024], F32)
            sc_ps1 = pst(stk, "sc_ps1", [128, 1024], F32)
            avA = pst(stk, "avA", [128, 512], F32)
            avB = pst(stk, "avB", [128, 512], F32)
            trp = pst(stk, "trp", [128, 512], BF16)
            ffn_ps = pst(stk, "ffn_ps", [128, 512], F32)
            sc_ps = (sc_ps0, sc_ps1)
            for qn in (GQ, YQ):
                P.wait(qn, k_qmm, 96)
                P.wait(qn, k_qev, 96)
                P.wait(qn, k_rope, 80)
                P.wait(qn, k_trv, 64)
                P.wait(qn, k_vev, 16)
            dma(YQ, masks_sb[:, :], masks, d_misc)
            for i in range(4):
                dma(GQ, x_own_sb[:, :].rearrange("p (kt n) -> p kt n", n=1024)[:, i * 8:(i + 1) * 8, :],
                    tiled(x_own, slice(None), i * 8, (i + 1) * 8, 1024), d_xo)
            for i in range(4):
                dma(GQ, Acat_sb[:, :].rearrange("p (kt n) -> p kt n", n=128)[:, i * 8:(i + 1) * 8, :],
                    tiled(AcatT, slice(None), i * 8, (i + 1) * 8, 128), d_ac)
            dma(GQ, Bcat_sb[:, :], Bcat, d_bc)
            dma(GQ, lwr_sb[:, :], lwr, d_fb)
            d_fb_lwr = d_fb.n
            dma(GQ, ffn_auglhs[1:2, :], fbias, d_fb)
            d_fb_all = d_fb.n
            P.wait(VQ, d_bc, 16)
            P.wait(VQ, d_fb, d_fb_lwr)
            vop(VQ, "tensor_scalar", Bcat_sb[:, :], Bcat_sb[:, :], lwr_sb[:, 0:1], 2.0,
                ALU.mult, ALU.mult, inc=k_hsev)

            P.wait(TQ, k_qev, 96)
            P.wait(TQ, k_rbev, 16)
            P.wait(TQ, k_vev, 16)
            P.wait(TQ, k_rope, 80)

            ffn_items = []
            if p2_ffn:
                def it_wcA():
                    P.wait(TQ, d_ac, 64)
                    for kt in range(NKT):
                        mm(ffn_ps[0:1, 384:512], ones_col_b[:, :],
                           Acat_sb[:, kt * 128:(kt + 1) * 128],
                           kt == 0, kt == NKT - 1, inc=(k_wcA if kt == NKT - 1 else None))
                    P.wait(VQ, k_wcA, 1)
                    P.wait(VQ, k_init, 1)
                    vop(VQ, "tensor_copy", ha_auglhs[0:1, :], ffn_ps[0:1, 384:512],
                        inc=k_haev)
                ffn_items.append(it_wcA)

                def it_h(fch):
                    P.wait(TQ, d_xo, 64)
                    P.wait(TQ, k_haev, 1)
                    P.wait(TQ, d_st, d_st_aug2o)
                    P.wait(TQ, k_hsev, 1 + fch)
                    for kt in range(NKT):
                        mm(ffn_ps[:, :], Acat_sb[:, kt * 128:(kt + 1) * 128],
                           x_own_sb[:, kt * 1024 + fch * 512:kt * 1024 + (fch + 1) * 512],
                           kt == 0, False)
                    mm(ffn_ps[:, :], ha_auglhs[:, :], aug2o[:, fch * 512:(fch + 1) * 512],
                       False, True, inc=k_hmm)
                    P.wait(VQ, k_hmm, fch + 1)
                    vop(VQ, "tensor_mul", hsT[:, fch * 512:(fch + 1) * 512], ffn_ps[:, :],
                        rb_own[:, fch * 512:(fch + 1) * 512], inc=k_hsev)
                ffn_items.append(lambda: it_h(0))
                ffn_items.append(lambda: it_h(1))

                def it_load(ot):
                    ws = wW_sb[:, (ot % 2) * NKT * 128:(ot % 2 + 1) * NKT * 128]
                    P.wait(GQ, k_ffn, 2 * (ot - 2) + 2)
                    for i in range(4):
                        dma(GQ, ws.rearrange("p (kt n) -> p kt n", n=128)[:, i * 8:(i + 1) * 8, :],
                            tiled(WT, slice(ot * 128, (ot + 1) * 128), i * 8, (i + 1) * 8, 128),
                            d_wW)

                def it_wc(ot):
                    ws = wW_sb[:, (ot % 2) * NKT * 128:(ot % 2 + 1) * NKT * 128]
                    it_load(ot + 2) if ot + 2 < NKT else None
                    P.wait(TQ, d_wW, 64 * (ot + 1))
                    P.wait(TQ, k_hsev, 3)
                    P.wait(TQ, k_gev, 2 * ot)
                    P.wait(TQ, k_wcev, ot)
                    for kt in range(NKT):
                        mm(ffn_ps[0:1, 384:512], ones_col_b[:, :], ws[:, kt * 128:(kt + 1) * 128],
                           kt == 0, kt == NKT - 1, inc=(k_wcf if kt == NKT - 1 else None))
                    P.wait(VQ, k_wcf, ot + 1)
                    vop(VQ, "tensor_copy", ffn_auglhs[0:1, ot * 128:(ot + 1) * 128],
                        ffn_ps[0:1, 384:512], inc=k_wcev)

                def it_g(ot, fch):
                    ws = wW_sb[:, (ot % 2) * NKT * 128:(ot % 2 + 1) * NKT * 128]
                    e_ = 2 * ot + fch
                    P.wait(TQ, k_wcev, ot + 1)
                    P.wait(TQ, d_bc, 16)
                    P.wait(TQ, d_fb, d_fb_all)
                    P.wait(TQ, k_gev, e_)
                    for kt in range(NKT):
                        mm(ffn_ps[:, :], ws[:, kt * 128:(kt + 1) * 128],
                           x_own_sb[:, kt * 1024 + fch * 512:kt * 1024 + (fch + 1) * 512],
                           kt == 0, False)
                    mm(ffn_ps[:, :], ffn_auglhs[:, ot * 128:(ot + 1) * 128],
                       aug2o[:, fch * 512:(fch + 1) * 512], False, False)
                    mm(ffn_ps[:, :], Bcat_sb[:, ot * 128:(ot + 1) * 128],
                       hsT[:, fch * 512:(fch + 1) * 512], False, True, inc=k_ffn)
                    P.wait(VQ, k_ffn, e_ + 1)
                    P.wait(VQ, d_gw, 16 * (e_ - 1))
                    vop(VQ, "tensor_mul", g_tile[:, (e_ % 2) * 512:(e_ % 2 + 1) * 512],
                        ffn_ps[:, :], rb_own[:, fch * 512:(fch + 1) * 512], inc=k_gev)
                    P.wait(YQ, k_gev, e_ + 1)
                    dma(YQ, g_scr.ap()[ot * 128:(ot + 1) * 128, fch * 512:(fch + 1) * 512],
                        g_tile[:, (e_ % 2) * 512:(e_ % 2 + 1) * 512], d_gw)

                it_load(0)
                it_load(1)
                for ot in range(NKT):
                    ffn_items.append(lambda ot=ot: it_wc(ot))
                    ffn_items.append(lambda ot=ot: it_g(ot, 0))
                    ffn_items.append(lambda ot=ot: it_g(ot, 1))
            n_items = len(ffn_items)
            items_done = 0

            G = 0
            uend = [0, 0]
            for u in range(p2_units):
                h, qc = u // NCH, u % NCH
                if True:
                    nkt = 4 * (qc + 1)
                    ngr = nkt // 2
                    qb = qblk[:, (u % 2) * 512:(u % 2 + 1) * 512]
                    P.wait(YQ, d_q, 16 * 64)
                    if u >= 2:
                        P.wait(YQ, k_sc, uend[u % 2])
                    dma(YQ, qb, q_scr.ap()[h * 128:(h + 1) * 128, qc * 512:(qc + 1) * 512], d_qb)
                    P.wait(TQ, d_qb, 16 * (u + 1))

                    mskcnt = {}

                    def emit_sc(gi, G_):
                        par = G_ % 2
                        sp = sc_ps[par]
                        P.wait(TQ, k_exp, G_ - 1)
                        for j in (0, 1):
                            kt = 2 * gi + j
                            mm(sp[:, j * 512:(j + 1) * 512], kT[:, kt * 128:(kt + 1) * 128],
                               qb, True, True, inc=(k_sc if j == 1 else None))
                        P.wait(SQ, k_sc, G_ + 1)
                        P.wait(SQ, k_avg, G_ - 1)
                        act(SQ, pT[:, par * 1024:(par + 1) * 1024], sp[:, :], AF.Exp,
                            f(SM_SCALE), f(SM_SHIFT), inc=k_exp)
                        if gi >= 2 * qc:
                            P.wait(VQ, d_misc, 48)
                            P.wait(VQ, k_exp, G_ + 1)
                            for j in (0, 1):
                                m = 2 * gi + j - 4 * qc
                                sl = pT[:, par * 1024 + j * 512:par * 1024 + (j + 1) * 512]
                                vop(VQ, "tensor_mul", sl, sl, masks_sb[:, m * 512:(m + 1) * 512],
                                    inc=(k_msk if j == 1 else None))
                            mskcnt[G_] = k_msk.n
                        else:
                            mskcnt[G_] = None

                    def emit_av(gi, G_):
                        par = G_ % 2
                        if mskcnt[G_] is not None:
                            P.wait(TQ, k_msk, mskcnt[G_])
                        else:
                            P.wait(TQ, k_exp, G_ + 1)
                        if u >= 1 and gi == 0:
                            P.wait(TQ, k_div, u)
                        for j in (0, 1):
                            kt = 2 * gi + j
                            for sub in range(4):
                                tgt = avA if sub < 2 else avB
                                col = 129 * (sub % 2)
                                mm(tgt[:, col:col + 129],
                                   pT[:, par * 1024 + j * 512 + sub * 128:par * 1024 + j * 512 + (sub + 1) * 128],
                                   vaug[:, 130 * kt:130 * kt + 129],
                                   kt == 0 and sub % 2 == 0, kt == nkt - 1,
                                   inc=(k_avg if (j == 1 and sub == 3) else None))

                    Gbase = G
                    for gi in range(ngr):
                        emit_sc(gi, Gbase + gi)
                        if gi >= 1:
                            emit_av(gi - 1, Gbase + gi - 1)
                    emit_av(ngr - 1, Gbase + ngr - 1)
                    G = Gbase + ngr
                    uend[u % 2] = G
                    P.wait(VQ, k_avg, G)
                    P.wait(VQ, k_div, u)
                    for s_ in range(4):
                        tgt = avA if s_ < 2 else avB
                        col = 129 * (s_ % 2)
                        vop(VQ, "reciprocal", rc[:, s_:s_ + 1], tgt[:, col + 128:col + 129],
                            inc=(k_rc if s_ == 3 else None))
                    P.wait(SQ, k_rc, u + 1)
                    P.wait(SQ, k_trm, 4 * u)
                    for s_ in range(4):
                        tgt = avA if s_ < 2 else avB
                        col = 129 * (s_ % 2)
                        act(SQ, att_nat[:, s_ * 128:(s_ + 1) * 128], tgt[:, col:col + 128],
                            AF.Identity, rc[:, s_:s_ + 1], f(0.0),
                            inc=(k_div if s_ == 3 else None))
                    P.wait(VQ, d_att, 16 * h)
                    P.wait(TQ, k_div, u + 1)
                    P.wait(TQ, k_trev, u)
                    for s_ in range(4):
                        tr(trp[:, s_ * 128:(s_ + 1) * 128],
                           att_nat[:, s_ * 128:(s_ + 1) * 128], inc=k_trm)
                    P.wait(VQ, k_trm, 4 * (u + 1))
                    vop(VQ, "tensor_copy", attst[:, qc * 512:(qc + 1) * 512], trp[:, :],
                        inc=k_trev)
                want_items = G * n_items // 1088 if p2_units == 64 else 0
                while items_done < want_items:
                    ffn_items[items_done]()
                    items_done += 1
                if qc == NCH - 1:
                    P.wait(SQ, k_trev, 16 * (h + 1))
                    dma(SQ, ag_in.ap()[h * 128:(h + 1) * 128, :], attst[:, :], d_att)
            while items_done < n_items and p2_units == 64:
                ffn_items[items_done]()
                items_done += 1
            assert p2_units < 64 or (G == 1088 and k_avg.n == 1088)
            if p2_ag:
                P.wait(GQ, d_att, 64)
                if p2_ffn:
                    P.wait(GQ, d_wW, d_wW.n)
                    P.wait(GQ, d_gw, 16 * 64)
                P.wait(GQ, d_qb, 16 * 64)
                P.wait(GQ, d_xo, 64)
                P.wait(GQ, d_ac, 64)
                P.wait(GQ, d_bc, 16)
                P.wait(GQ, d_fb, d_fb_all)
                c_ag2.n += 1
                P.emit(GQ, lambda e: e.collective_compute(
                    "AllGather", ALU.bypass, replica_groups=[list(range(8))],
                    ins=[ag_in.ap().opt()], outs=[ag_out.ap().opt()]).then_inc(c_ag2.h, 1))

            assert (not p2_ffn) or (k_ffn.n == 64 and items_done == n_items)
            if dbg and phases == 3:
                P.wait(YQ, k_trev, p2_units)
                dma(YQ, dbg_kT.ap()[0:128, 0:S], attst[:, :], d_out)
                if p2_units == 64:
                    P.wait(YQ, d_att, 16 * 4)
                    dma(YQ, dbg_agin.ap()[:, :], ag_in.ap()[:, :], d_out)
                if p2_ag:
                    P.wait(YQ, c_ag2, 1)
                    dma(YQ, dbg_agout.ap()[:, :], ag_out.ap()[:, :], d_out)
                if p2_ffn:
                    P.wait(YQ, d_gw, 16 * 64)
                    P.wait(YQ, k_hsev, 3)
                    P.wait(YQ, k_wcev, 32)
                    dma(YQ, dbg_g.ap()[:, :], g_scr.ap()[:, :], d_out)
                    dma(YQ, dbg_hs.ap()[:, :], hsT[:, :], d_out)
                    dma(YQ, dbg_aug2.ap()[:, 0:D], ffn_auglhs[:, :], d_out)
                P.wait(YQ, d_out, d_out.n)
            P.flush(block)

        if phases <= 3:
            st.close()
            return nc

        with contextlib.ExitStack() as stk:
            attg = sb(stk, "attg", [128, NKT * 1024], BF16)
            op_auglhs = sb(stk, "op_auglhs", [1, D], BF16)
            wo_sb = sb(stk, "wo_sb", [128, 2 * NKT * 128], BF16)
            g2 = sb(stk, "g2", [128, 2 * 512], BF16)
            silu_sb = sb(stk, "silu_sb", [128, 2 * 512], F32)
            xr = sb(stk, "xr", [128, 2 * 512], F32)
            tmp_ep = sb(stk, "tmp_ep", [128, 512], F32)
            out_sb = sb(stk, "out_sb", [128, 2 * 512], F32)
            op_ps0 = pst(stk, "op_ps0", [128, 512], F32)
            op_ps1 = pst(stk, "op_ps1", [128, 512], F32)
            op_ps = (op_ps0, op_ps1)
            for qn in (GQ, YQ):
                P.wait(qn, k_gev, 64)
                P.wait(qn, d_gw, 16 * 64)
                P.wait(qn, k_exp, 1088)
                P.wait(qn, k_trev, 64)
                P.wait(qn, d_att, 16 * 4)
                P.wait(qn, c_ag2, 1)
            dma(GQ, op_auglhs[0:1, :], b_o, d_fb)
            d_fb_bo = d_fb.n
            P.wait(GQ, c_ag2, 1)
            _cache = {}

            def load_attg(e, _cache=_cache):
                if "pid" not in _cache:
                    _cache["pid"] = nc.partition_id([ET.Pool])
                pid = _cache["pid"]
                for i in range(4):
                    src = ag_out.ap()[i * 1024:(i + 1) * 1024, :].rearrange(
                        "(kt p) q -> p kt q", p=128)[:, :, bass.ts(pid, OWN)]
                    dst = attg[:, :].rearrange("p (kt n) -> p kt n", n=OWN)[:, i * 8:(i + 1) * 8, :]
                    e.dma_start(out=dst, in_=src).then_inc(d_agl.h, 16)
            d_agl.n += 64
            P.emit(GQ, load_attg)

            P.wait(TQ, k_exp, 1088)
            P.wait(TQ, k_msk, k_msk.n)
            P.wait(TQ, k_gev, 64)
            for ot in range(NKT):
                par = ot % 2
                ws = wo_sb[:, par * NKT * 128:(par + 1) * NKT * 128]
                P.wait(GQ, k_op, 2 * (ot - 2) + 2)
                for i in range(4):
                    dma(GQ, ws.rearrange("p (kt n) -> p kt n", n=128)[:, i * 8:(i + 1) * 8, :],
                        tiled(woT, slice(ot * 128, (ot + 1) * 128), i * 8, (i + 1) * 8, 128), d_wo)
                P.wait(TQ, d_agl, 64)
                P.wait(TQ, d_wo, 64 * (ot + 1))
                P.wait(TQ, d_fb, d_fb_bo)
                for fch in (0, 1):
                    e_ = 2 * ot + fch
                    par2 = e_ % 2
                    P.wait(YQ, d_gw, 16 * 64)
                    P.wait(YQ, k_silu, e_ - 1)
                    dma(YQ, g2[:, par2 * 512:(par2 + 1) * 512],
                        g_scr.ap()[ot * 128:(ot + 1) * 128, fch * 512:(fch + 1) * 512], d_g)
                    P.wait(YQ, k_ep, e_ - 1)
                    dma(YQ, xr[:, par2 * 512:(par2 + 1) * 512],
                        x_own[ot * 128:(ot + 1) * 128, fch * 512:(fch + 1) * 512], d_xr)
                    P.wait(TQ, k_ep, e_ - 1)
                    for kt in range(NKT):
                        mm(op_ps[par2][:, :], ws[:, kt * 128:(kt + 1) * 128],
                           attg[:, kt * 1024 + fch * 512:kt * 1024 + (fch + 1) * 512],
                           kt == 0, False)
                    mm(op_ps[par2][:, :], op_auglhs[0:1, ot * 128:(ot + 1) * 128],
                       ones_row_b[0:1, :], False, True, inc=k_op)
                    P.wait(SQ, d_g, 16 * (e_ + 1))
                    P.wait(SQ, k_ep, e_ - 1)
                    act(SQ, silu_sb[:, par2 * 512:(par2 + 1) * 512],
                        g2[:, par2 * 512:(par2 + 1) * 512], AF.Silu, f(1.0), f(0.0),
                        inc=k_silu)
                    P.wait(VQ, k_op, e_ + 1)
                    P.wait(VQ, k_silu, e_ + 1)
                    P.wait(VQ, d_xr, 16 * (e_ + 1))
                    P.wait(VQ, d_out, 16 * (e_ - 1))
                    vop(VQ, "tensor_mul", tmp_ep[:, :], op_ps[par2][:, :],
                        silu_sb[:, par2 * 512:(par2 + 1) * 512], inc=k_v)
                    P.wait(VQ, k_v, k_v.n)
                    vop(VQ, "tensor_add", out_sb[:, par2 * 512:(par2 + 1) * 512],
                        tmp_ep[:, :], xr[:, par2 * 512:(par2 + 1) * 512], inc=k_ep)
                    P.wait(YQ, k_ep, e_ + 1)
                    dma(YQ, outT[ot * 128:(ot + 1) * 128, fch * 512:(fch + 1) * 512],
                        out_sb[:, par2 * 512:(par2 + 1) * 512], d_out)
            assert k_op.n == 64 and d_out.n == 16 * 64
            P.wait(YQ, d_out, 16 * 64)
            P.flush(block)

    st.close()
    return nc


def _host_stage(inputs):
    x = np.asarray(inputs["L0_attn_x"], dtype=np.float32).reshape(S, D)
    w_qkv = np.asarray(inputs["L0_attn_w_qkv"], dtype=np.float32)
    b_qkv = np.asarray(inputs["L0_attn_b_qkv"], dtype=np.float32)
    w_o = np.asarray(inputs["L0_attn_w_o"], dtype=np.float32)
    b_o = np.asarray(inputs["L0_attn_b_o"], dtype=np.float32)
    lora_A = np.asarray(inputs["L0_ffn_lora_A"], dtype=np.float32)
    lora_B = np.asarray(inputs["L0_ffn_lora_B"], dtype=np.float32)
    W = np.asarray(inputs["L0_ffn_W"], dtype=np.float32)
    fbias = np.asarray(inputs["L0_ffn_bias"], dtype=np.float32)
    lw = np.asarray(inputs["L0_ffn_lora_weights"], dtype=np.float32)

    xT = np.ascontiguousarray(x.T)
    woT = np.ascontiguousarray(w_o.T)
    WT = np.ascontiguousarray(W.T)
    AcatT = np.ascontiguousarray(lora_A.reshape(128, D).T)
    Bcat = np.ascontiguousarray(lora_B.transpose(0, 2, 1).reshape(128, D))
    lwr = np.ascontiguousarray(np.repeat(lw, 16).reshape(128, 1))

    inv_freq = (1.0 / 10000.0 ** (np.arange(0, HD, 2, dtype=np.float64) / HD))
    ang = inv_freq[:, None] * np.arange(S, dtype=np.float64)[None, :]
    cos64 = np.cos(ang).astype(np.float32)
    sin64 = np.sin(ang).astype(np.float32)
    cosT = np.ascontiguousarray(np.vstack([cos64, cos64]))
    sinT = np.ascontiguousarray(np.vstack([-sin64, sin64]))
    swap = np.zeros((128, 128), dtype=np.float32)
    swap[(np.arange(128) + 64) % 128, np.arange(128)] = 1.0
    swap = swap.astype(ml_dtypes.bfloat16)

    kk = np.arange(128)[:, None]
    qq = np.arange(512)[None, :]
    mk = np.concatenate([( (m * 128 + kk) <= qq).astype(np.float32) for m in range(4)],
                        axis=1)
    masks = mk.astype(ml_dtypes.bfloat16)
    ident = np.eye(128, dtype=ml_dtypes.bfloat16)

    def even_odd(base):
        return [base + 2 * j for j in range(64)] + [base + 2 * j + 1 for j in range(64)]

    in_maps = []
    for c in range(8):
        rows = []
        for hh in range(4 * c, 4 * c + 4):
            rows += even_odd(128 * hh)
        rows += even_odd(4096 + 128 * c)
        rows += list(range(5120 + 128 * c, 5120 + 128 * (c + 1)))
        rows = np.asarray(rows)
        wqkvT = np.ascontiguousarray(w_qkv[rows, :].T)
        bq = np.ascontiguousarray(b_qkv[rows].reshape(1, QKVO))
        in_maps.append({
            "xT": xT,
            "x_own": np.ascontiguousarray(xT[:, c * OWN:(c + 1) * OWN]),
            "wqkvT": wqkvT,
            "bqkv": bq,
            "woT": woT,
            "WT": WT,
            "AcatT": AcatT,
            "Bcat": Bcat,
            "b_o": np.ascontiguousarray(b_o.reshape(1, D)),
            "fbias": np.ascontiguousarray(fbias.reshape(1, D)),
            "lwr": lwr,
            "cosT": cosT,
            "sinT": sinT,
            "masks": masks,
            "ident": ident,
            "swapm": swap,
        })
    return in_maps


def kernel(**inputs):
    in_maps = _host_stage(inputs)
    nc = build_nc()
    res = run_bass_kernel_spmd(nc, in_maps, core_ids=list(range(8)))
    cols = [res.results[c]["outT"] for c in range(8)]
    full = np.concatenate(cols, axis=1)
    return np.ascontiguousarray(full.T).reshape(1, S, D).astype(np.float32)


# version 17
# speedup vs baseline: 1.1309x; 1.1309x over previous
import contextlib
import math

import numpy as np
import ml_dtypes

import concourse.bass as bass
import concourse.mybir as mybir
from concourse.bass_utils import run_bass_kernel_spmd

F32 = mybir.dt.float32
BF16 = mybir.dt.bfloat16
AF = mybir.ActivationFunctionType
ALU = mybir.AluOpType
ET = mybir.EngineType

S = 8192
D = 4096
HD = 128
NKT = D // 128
NCH = S // 512
OWN = S // 8
QH = 4
QKVO = 768
EPS = 1e-5
SM_SCALE = 1.0 / math.sqrt(128.0)
SM_SHIFT = -20.0
f = np.float32

TQ, VQ, SQ, GQ, YQ = "t", "v", "s", "g", "y"


class Sem:
    __slots__ = ("h", "n", "name")

    def __init__(self, h, name):
        self.h = h
        self.n = 0
        self.name = name


class Prog:

    def __init__(self):
        self.q = {k: [] for k in (TQ, VQ, SQ, GQ, YQ)}
        self.hw = {k: {} for k in (TQ, VQ, SQ, GQ, YQ)}

    def emit(self, qn, fn):
        self.q[qn].append(fn)

    def wait(self, qn, sem, val):
        if val <= 0:
            return
        if val <= self.hw[qn].get(sem.name, 0):
            return
        self.hw[qn][sem.name] = val
        self.q[qn].append(lambda e, h=sem.h, v=val: e.wait_ge(h, v))

    def flush(self, block):
        attr = {TQ: "tensor", VQ: "vector", SQ: "scalar", GQ: "gpsimd", YQ: "sync"}
        for qn, name in attr.items():
            ops = self.q[qn]
            if not ops:
                continue
            self.q[qn] = []

            def body(e, ops=ops):
                for op in ops:
                    op(e)

            getattr(block, name)(body)


def build_nc(phases=4, dbg=False, p2_units=64, p2_ag=True, p2_ffn=True):
    nc = bass.Bass("TRN2", target_bir_lowering=False)

    xT = nc.dram_tensor("xT", [D, S], F32, kind="ExternalInput").ap()
    x_own = nc.dram_tensor("x_own", [D, OWN], F32, kind="ExternalInput").ap()
    wqkvT = nc.dram_tensor("wqkvT", [D, QKVO], F32, kind="ExternalInput").ap()
    bqkv = nc.dram_tensor("bqkv", [1, QKVO], F32, kind="ExternalInput").ap()
    woT = nc.dram_tensor("woT", [D, D], F32, kind="ExternalInput").ap()
    WT = nc.dram_tensor("WT", [D, D], F32, kind="ExternalInput").ap()
    AcatT = nc.dram_tensor("AcatT", [D, 128], F32, kind="ExternalInput").ap()
    Bcat = nc.dram_tensor("Bcat", [128, D], F32, kind="ExternalInput").ap()
    b_o = nc.dram_tensor("b_o", [1, D], F32, kind="ExternalInput").ap()
    fbias = nc.dram_tensor("fbias", [1, D], F32, kind="ExternalInput").ap()
    lwr = nc.dram_tensor("lwr", [128, 1], F32, kind="ExternalInput").ap()
    cosT = nc.dram_tensor("cosT", [128, S], F32, kind="ExternalInput").ap()
    sinT = nc.dram_tensor("sinT", [128, S], F32, kind="ExternalInput").ap()
    swapm = nc.dram_tensor("swapm", [128, 128], BF16, kind="ExternalInput").ap()
    masks = nc.dram_tensor("masks", [128, 4 * 512], BF16, kind="ExternalInput").ap()
    ident = nc.dram_tensor("ident", [128, 128], BF16, kind="ExternalInput").ap()

    q_scr = nc.dram_tensor("q_scr", [512, S], BF16)
    g_scr = nc.dram_tensor("g_scr", [D, OWN], BF16)
    ag_in = nc.dram_tensor("ag_in", [512, S], BF16)
    ag_out = nc.dram_tensor("ag_out", [D, S], BF16, addr_space="Shared")
    st_in = nc.dram_tensor("st_in", [3, OWN], F32)
    st_out = nc.dram_tensor("st_out", [24, OWN], F32)

    outT = nc.dram_tensor("outT", [D, OWN], F32, kind="ExternalOutput").ap()
    if dbg:
        dbg_st = nc.dram_tensor("dbg_st", [24, OWN], F32, kind="ExternalOutput")
        dbg_aug2 = nc.dram_tensor("dbg_aug2", [2, S], BF16, kind="ExternalOutput")
        dbg_aug2o = nc.dram_tensor("dbg_aug2o", [2, OWN], BF16, kind="ExternalOutput")
        dbg_rb = nc.dram_tensor("dbg_rb", [128, OWN], F32, kind="ExternalOutput")
        dbg_q = nc.dram_tensor("dbg_q", [512, S], BF16, kind="ExternalOutput")
        dbg_kT = nc.dram_tensor("dbg_kT", [128, S], BF16, kind="ExternalOutput")
        dbg_va = nc.dram_tensor("dbg_va", [128, 64 * 130], BF16, kind="ExternalOutput")
        dbg_agin = nc.dram_tensor("dbg_agin", [512, S], BF16, kind="ExternalOutput")
        dbg_agout = nc.dram_tensor("dbg_agout", [D, S], BF16, kind="ExternalOutput")
        dbg_g = nc.dram_tensor("dbg_g", [D, OWN], BF16, kind="ExternalOutput")
        dbg_hs = nc.dram_tensor("dbg_hs", [128, OWN], BF16, kind="ExternalOutput")

    P = Prog()
    st = contextlib.ExitStack()

    def sem(name):
        return Sem(st.enter_context(nc.semaphore(name)), name)

    d_wq, d_x, d_xs, d_cs, d_misc = sem("d_wq"), sem("d_x"), sem("d_xs"), sem("d_cs"), sem("d_misc")
    d_q, d_qb, d_st, d_fb = sem("d_q"), sem("d_qb"), sem("d_st"), sem("d_fb")
    d_wW, d_ac, d_bc, d_att = sem("d_wW"), sem("d_ac"), sem("d_bc"), sem("d_att")
    d_agl, d_wo, d_g, d_gw, d_xr, d_out, d_xo = (
        sem("d_agl"), sem("d_wo"), sem("d_g"), sem("d_gw"), sem("d_xr"), sem("d_out"), sem("d_xo"))
    c_ag1, c_ag2 = sem("c_ag1"), sem("c_ag2")
    k_init, k_sq, k_sumA, k_sumB, k_strow, k_s = (
        sem("k_init"), sem("k_sq"), sem("k_sumA"), sem("k_sumB"), sem("k_strow"), sem("k_s"))
    k_rbo = sem("k_rbo")
    k_wcq, k_wcqev, k_qmm, k_qev, k_rbmm, k_rbev, k_rope, k_trv, k_vev = (
        sem("k_wcq"), sem("k_wcqev"), sem("k_qmm"), sem("k_qev"), sem("k_rbmm"),
        sem("k_rbev"), sem("k_rope"), sem("k_trv"), sem("k_vev"))
    k_sc, k_exp, k_msk, k_avg, k_rc, k_div, k_trm, k_trev = (
        sem("k_sc"), sem("k_exp"), sem("k_msk"), sem("k_avg"), sem("k_rc"),
        sem("k_div"), sem("k_trm"), sem("k_trev"))
    k_wcA, k_haev, k_hmm, k_hsev, k_wcf, k_wcev, k_ffn, k_gev = (
        sem("k_wcA"), sem("k_haev"), sem("k_hmm"), sem("k_hsev"), sem("k_wcf"),
        sem("k_wcev"), sem("k_ffn"), sem("k_gev"))
    k_op, k_silu, k_v, k_ep = sem("k_op"), sem("k_silu"), sem("k_v"), sem("k_ep")
    k_swp = sem("k_swp")
    d_wb = sem("d_wb")

    def sb(stk, name, shape, dtype):
        return stk.enter_context(nc.sbuf_tensor(name, shape, dtype))

    def pst(stk, name, shape, dtype):
        return stk.enter_context(nc.psum_tensor(name, shape, dtype))

    def mm(out_ap, lhsT, rhs, start, stop, inc=None, inck=1):
        if inc is not None:
            inc.n += inck

        def fn(e, o=out_ap, l=lhsT, r=rhs, a=start, b=stop, i=inc, k=inck):
            ins = e.matmul(o, l, r, start=a, stop=b, skip_group_check=True)
            if i is not None:
                ins.then_inc(i.h, k)
        P.emit(TQ, fn)

    def tr(out_ap, in_ap, inc=None):
        if inc is not None:
            inc.n += 1

        def fn(e, o=out_ap, x=in_ap, i=inc):
            ins = e.transpose(o, x, id_sb[:, :])
            if i is not None:
                ins.then_inc(i.h, 1)
        P.emit(TQ, fn)

    def vop(qn, meth, *args, inc=None, inck=1, **kw):
        if inc is not None:
            inc.n += inck

        def fn(e, m=meth, a=args, i=inc, k=inck, kw=kw):
            ins = getattr(e, m)(*a, **kw)
            if i is not None:
                ins.then_inc(i.h, k)
        P.emit(qn, fn)

    def act(qn, out_ap, in_ap, func, scale, bias, inc=None, accum=None):
        if inc is not None:
            inc.n += 1

        def fn(e, o=out_ap, x=in_ap, fu=func, sc=scale, bi=bias, i=inc, ac=accum):
            kw = {}
            if ac is not None:
                kw["accum_out"] = ac
            ins = e.activation(o, x, fu, bias=bi, scale=sc, **kw)
            if i is not None:
                ins.then_inc(i.h, 1)
        P.emit(qn, fn)

    def dma(qn, out_ap, in_ap, inc):
        inc.n += 16

        def fn(e, o=out_ap, x=in_ap, i=inc):
            e.dma_start(out=o, in_=x).then_inc(i.h, 16)
        P.emit(qn, fn)

    def tiled(dram_ap, cols, kt0, kt1, width):
        return dram_ap[kt0 * 128:kt1 * 128, cols].rearrange("(kt p) n -> p kt n", p=128)

    if True:
        kT = sb(st, "kT", [128, S], BF16)
        vaug = sb(st, "vaug", [128, 64 * 130], BF16)
        aug2o = sb(st, "aug2o", [2, OWN], BF16)
        rb_own = sb(st, "rb_own", [128, OWN], F32)
        id_sb = sb(st, "id_sb", [128, 128], BF16)
        swapm_sb = sb(st, "swapm_sb", [128, 128], BF16)
        ones_col_f = sb(st, "ones_col_f", [128, 1], F32)
        ones_row_f = sb(st, "ones_row_f", [1, 128], F32)
        ones_col_b = sb(st, "ones_col_b", [128, 1], BF16)
        ones_row_b = sb(st, "ones_row_b", [1, 512], BF16)
        ha_auglhs = sb(st, "ha_auglhs", [2, 128], BF16)
        ffn_auglhs = sb(st, "ffn_auglhs", [2, D], BF16)
        hsT = sb(st, "hsT", [128, OWN], BF16)
        block = st.enter_context(nc.Block())
        vop(VQ, "memset", ones_col_f[:, :], 1.0)
        vop(VQ, "memset", ones_row_f[:, :], 1.0)
        vop(VQ, "memset", ones_col_b[:, :], 1.0)
        vop(VQ, "memset", ones_row_b[:, :], 1.0)
        vop(VQ, "memset", ha_auglhs[:, :], 0.0)
        vaug_t = vaug[:, :].rearrange("p (t c) -> p t c", c=130)
        vop(VQ, "memset", vaug_t[:, :, 128:129], 1.0, inc=k_init)

        dma(YQ, id_sb[:, :], ident, d_misc)
        dma(YQ, swapm_sb[:, :], swapm, d_misc)

        stkA = st.enter_context(contextlib.ExitStack())
        aug2 = sb(stkA, "aug2", [2, S], BF16)
        qkv_auglhs = sb(stkA, "qkv_auglhs", [2, QKVO], BF16)
        r_chunk = sb(stkA, "r_chunk", [1, 2 * 512], F32)
        with contextlib.ExitStack() as stk:
            xf32 = sb(stk, "xf32", [128, 2 * OWN], F32)
            xsq = sb(stk, "xsq", [128, 2 * OWN], F32)
            negm_l = sb(stk, "negm_l", [1, OWN], F32)
            invr_l = sb(stk, "invr_l", [1, OWN], F32)
            r_l = sb(stk, "r_l", [1, OWN], F32)
            lnrow = sb(stk, "lnrow", [1, OWN], F32)
            ex2 = sb(stk, "ex2", [1, OWN], F32)
            m2 = sb(stk, "m2", [1, OWN], F32)
            sx_ps = pst(stk, "sx_ps", [1, OWN], F32)
            sx2_ps = pst(stk, "sx2_ps", [1, OWN], F32)
            rbo_ps = pst(stk, "rbo_ps", [128, OWN], F32)
            for kt in range(NKT):
                par = kt % 2
                xf = xf32[:, par * OWN:(par + 1) * OWN]
                xq = xsq[:, par * OWN:(par + 1) * OWN]
                P.wait(YQ, k_sq, kt - 1)
                P.wait(YQ, k_sumA, kt - 1)
                dma(YQ, xf, x_own[kt * 128:(kt + 1) * 128, :], d_xs)
                P.wait(SQ, d_xs, 16 * (kt + 1))
                P.wait(SQ, k_sumB, kt - 1)
                act(SQ, xq, xf, AF.Square, f(1.0), f(0.0), inc=k_sq)
                P.wait(TQ, k_init, 1)
                P.wait(TQ, d_xs, 16 * (kt + 1))
                mm(sx_ps[0:1, 0:512], ones_col_f[:, :], xf[:, 0:512], kt == 0, kt == NKT - 1)
                mm(sx_ps[0:1, 512:1024], ones_col_f[:, :], xf[:, 512:1024], kt == 0, kt == NKT - 1,
                   inc=k_sumA)
                P.wait(TQ, k_sq, kt + 1)
                mm(sx2_ps[0:1, 0:512], ones_col_f[:, :], xq[:, 0:512], kt == 0, kt == NKT - 1)
                mm(sx2_ps[0:1, 512:1024], ones_col_f[:, :], xq[:, 512:1024], kt == 0, kt == NKT - 1,
                   inc=k_sumB)
            P.wait(VQ, k_sumA, NKT)
            P.wait(VQ, k_sumB, NKT)
            vop(VQ, "tensor_scalar_mul", negm_l[:, :], sx_ps[0:1, :], -1.0 / D)
            vop(VQ, "tensor_scalar_mul", ex2[:, :], sx2_ps[0:1, :], 1.0 / D, inc=k_v)
            P.wait(VQ, k_v, k_v.n)
            vop(VQ, "tensor_mul", m2[:, :], negm_l[:, :], negm_l[:, :], inc=k_v)
            P.wait(VQ, k_v, k_v.n)
            vop(VQ, "tensor_sub", m2[:, :], ex2[:, :], m2[:, :], inc=k_v)
            P.wait(SQ, k_v, k_v.n)
            act(SQ, lnrow[:, :], m2[:, :], AF.Ln, f(1.0), f(EPS), inc=k_s)
            P.wait(SQ, k_s, k_s.n)
            act(SQ, invr_l[:, :], lnrow[:, :], AF.Exp, f(0.5), f(0.0), inc=k_s)
            act(SQ, r_l[:, :], lnrow[:, :], AF.Exp, f(-0.5), f(0.0), inc=k_s)
            P.wait(TQ, k_s, k_s.n)
            mm(rbo_ps[:, 0:512], ones_row_f[:, :], r_l[0:1, 0:512], True, True)
            mm(rbo_ps[:, 512:1024], ones_row_f[:, :], r_l[0:1, 512:1024], True, True, inc=k_rbo)
            P.wait(VQ, k_rbo, 1)
            vop(VQ, "tensor_copy", rb_own[:, :], rbo_ps[:, :], inc=k_strow)
            P.wait(YQ, k_s, k_s.n)
            P.wait(YQ, k_v, k_v.n)
            dma(YQ, st_in.ap()[0:1, :], negm_l[:, :], d_st)
            dma(YQ, st_in.ap()[1:2, :], invr_l[:, :], d_st)
            dma(YQ, st_in.ap()[2:3, :], r_l[:, :], d_st)
            P.wait(GQ, d_st, 48)
            c_ag1.n += 1
            P.emit(GQ, lambda e: e.collective_compute(
                "AllGather", ALU.bypass, replica_groups=[list(range(8))],
                ins=[st_in.ap().opt()], outs=[st_out.ap().opt()]).then_inc(c_ag1.h, 1))
            P.wait(GQ, c_ag1, 1)
            for row in (0, 1):
                dma(GQ, aug2[row:row + 1, :].rearrange("p (rr n) -> p rr n", n=OWN),
                    st_out.ap()[row:24:3, :].rearrange("(o rr) n -> o rr n", o=1), d_st)
            d_st_aug2 = d_st.n
            dma(GQ, aug2o[0:1, :], negm_l[0:1, :], d_st)
            dma(GQ, aug2o[1:2, :], invr_l[0:1, :], d_st)
            d_st_aug2o = d_st.n
            if dbg and phases == 1:
                P.wait(GQ, d_st, d_st_aug2o)
                P.wait(GQ, k_strow, 1)
                dma(GQ, dbg_st.ap()[:, :], st_out.ap()[:, :], d_out)
                dma(GQ, dbg_aug2.ap()[:, :], aug2[:, :], d_out)
                dma(GQ, dbg_aug2o.ap()[:, :], aug2o[:, :], d_out)
                dma(GQ, dbg_rb.ap()[:, :], rb_own[:, :], d_out)
                P.wait(GQ, d_out, d_out.n)
            P.flush(block)

        if phases <= 1:
            st.close()
            return nc

        if True:
            stk = stkA
            wqkv_sb = sb(stk, "wqkv_sb", [128, NKT * QKVO], BF16)
            xchunk = sb(stk, "xchunk", [128, 2 * NKT * 512], BF16)
            csd = sb(stk, "csd", [128, 2 * 1024], F32)
            qkvb = sb(stk, "qkvb", [128, 2 * 512], BF16)
            ta = sb(stk, "ta", [128, 512], F32)
            tb = sb(stk, "tb", [128, 512], F32)
            qstage = sb(stk, "qstage", [128, 2 * 512], BF16)
            vT_stage = sb(stk, "vT_stage", [128, 512], BF16)
            rb_sb = sb(stk, "rb_sb", [128, 2 * 512], F32)
            qkv_ps0 = pst(stk, "qkv_ps0", [128, 512], F32)
            qkv_ps1 = pst(stk, "qkv_ps1", [128, 512], F32)
            rbq_ps = pst(stk, "rbq_ps", [128, 512], F32)
            trv_ps = pst(stk, "trv_ps", [128, 512], BF16)
            wcq_ps = pst(stk, "wcq_ps", [1, QKVO], F32)
            sw_ps = pst(stk, "sw_ps", [128, 2 * 512], F32)
            qkv_ps = (qkv_ps0, qkv_ps1)
            P.wait(GQ, k_sumA, NKT)
            P.wait(GQ, k_sumB, NKT)
            P.wait(GQ, k_sq, NKT)
            for i in range(4):
                dma(GQ, wqkv_sb[:, :].rearrange("p (kt o) -> p kt o", o=QKVO)[:, i * 8:(i + 1) * 8, :],
                    tiled(wqkvT, slice(None), i * 8, (i + 1) * 8, QKVO), d_wq)

            dma(GQ, qkv_auglhs[1:2, :], bqkv, d_fb)
            P.wait(TQ, k_strow, 1)
            P.wait(TQ, d_wq, 64)
            for ot in range(6):
                for kt in range(NKT):
                    mm(wcq_ps[0:1, ot * 128:(ot + 1) * 128], ones_col_b[:, :],
                       wqkv_sb[:, kt * QKVO + ot * 128:kt * QKVO + (ot + 1) * 128],
                       kt == 0, kt == NKT - 1, inc=(k_wcq if kt == NKT - 1 else None))
            P.wait(VQ, k_wcq, 6)
            vop(VQ, "tensor_copy", qkv_auglhs[0:1, :], wcq_ps[0:1, :], inc=k_wcqev)
            P.wait(TQ, d_fb, 16)

            n_qdma = 0
            for ch in range(NCH):
                par = ch % 2
                cols = slice(ch * 512, (ch + 1) * 512)
                xc = xchunk[:, par * NKT * 512:(par + 1) * NKT * 512]
                P.wait(GQ, k_qmm, 6 * (ch - 1))
                for i in range(4):
                    dma(GQ, xc.rearrange("p (kt n) -> p kt n", n=512)[:, i * 8:(i + 1) * 8, :],
                        tiled(xT, cols, i * 8, (i + 1) * 8, 512), d_x)
                P.wait(YQ, k_rope, 5 * (ch - 1))
                dma(YQ, csd[:, par * 1024:par * 1024 + 512], cosT[:, cols], d_cs)
                dma(YQ, csd[:, par * 1024 + 512:par * 1024 + 1024], sinT[:, cols], d_cs)
                rr, rcl = ch // 2, (ch % 2) * 512
                P.wait(YQ, c_ag1, 1)
                P.wait(YQ, k_rbmm, ch - 1)
                dma(YQ, r_chunk[0:1, par * 512:(par + 1) * 512],
                    st_out.ap()[3 * rr + 2:3 * rr + 3, rcl:rcl + 512], d_cs)
                P.wait(TQ, k_rbev, ch)
                P.wait(TQ, d_cs, 48 * (ch + 1))
                mm(rbq_ps[:, :], ones_row_f[:, :], r_chunk[0:1, par * 512:(par + 1) * 512],
                   True, True, inc=k_rbmm)
                P.wait(TQ, d_x, 64 * (ch + 1))
                P.wait(TQ, k_wcqev, 1)
                P.wait(TQ, d_st, d_st_aug2)
                for ot in range(6):
                    g_idx = 6 * ch + ot
                    qp = qkv_ps[g_idx % 2]
                    P.wait(TQ, k_qev, g_idx - 1)
                    for kt in range(NKT):
                        mm(qp[:, :], wqkv_sb[:, kt * QKVO + ot * 128:kt * QKVO + (ot + 1) * 128],
                           xc[:, kt * 512:(kt + 1) * 512], kt == 0, False)
                    mm(qp[:, :], qkv_auglhs[:, ot * 128:(ot + 1) * 128], aug2[:, cols],
                       False, True, inc=k_qmm)
                    P.wait(VQ, k_qmm, g_idx + 1)
                    if ot == 0:
                        P.wait(VQ, k_rbmm, ch + 1)
                        vop(VQ, "tensor_copy", rb_sb[:, par * 512:(par + 1) * 512],
                            rbq_ps[:, :], inc=k_rbev)
                        P.wait(VQ, k_rbev, ch + 1)
                    rbs = rb_sb[:, par * 512:(par + 1) * 512]
                    if ot == 5:
                        P.wait(VQ, k_trv, 4 * ch)
                        vop(VQ, "tensor_mul", vT_stage[:, :], qp[:, :], rbs, inc=k_qev)
                        continue
                    ri = 5 * ch + ot
                    qbs = qkvb[:, (ri % 2) * 512:(ri % 2 + 1) * 512]
                    P.wait(VQ, k_swp, ri - 1)
                    vop(VQ, "tensor_mul", qbs, qp[:, :], rbs, inc=k_qev)
                    P.wait(TQ, k_qev, k_qev.n)
                    P.wait(TQ, k_rope, ri - 1)
                    P.wait(TQ, d_misc, 32)
                    mm(sw_ps[:, (ri % 2) * 512:(ri % 2 + 1) * 512], swapm_sb[:, :], qbs,
                       True, True, inc=k_swp)
                    P.wait(VQ, d_cs, 48 * (ch + 1))
                    coss = csd[:, par * 1024:par * 1024 + 512]
                    sins = csd[:, par * 1024 + 512:par * 1024 + 1024]
                    if ot < 4:
                        dst = qstage[:, (ot % 2) * 512:(ot % 2 + 1) * 512]
                        P.wait(VQ, d_q, 16 * (4 * ch + ot - 1))
                    else:
                        dst = kT[:, cols]
                    vop(VQ, "tensor_mul", ta[:, :], qbs, coss)
                    P.wait(VQ, k_swp, ri + 1)
                    vop(VQ, "tensor_mul", tb[:, :],
                        sw_ps[:, (ri % 2) * 512:(ri % 2 + 1) * 512], sins, inc=k_v)
                    P.wait(VQ, k_v, k_v.n)
                    vop(VQ, "tensor_add", dst, ta[:, :], tb[:, :], inc=k_rope)
                    if ot < 4:
                        P.wait(YQ, k_rope, k_rope.n)
                        dma(YQ, q_scr.ap()[ot * 128:(ot + 1) * 128, cols], dst, d_q)
                        n_qdma += 1
                P.wait(TQ, k_qev, 6 * ch + 6)
                P.wait(TQ, k_vev, ch)
                P.wait(TQ, d_misc, 16)
                for i in range(4):
                    tr(trv_ps[:, i * 128:(i + 1) * 128], vT_stage[:, i * 128:(i + 1) * 128],
                       inc=k_trv)
                P.wait(SQ, k_trv, 4 * (ch + 1))
                act(SQ, vaug_t[:, 4 * ch:4 * ch + 4, 0:128],
                    trv_ps[:, :].rearrange("p (t c) -> p t c", c=128), AF.Identity,
                    f(1.0), f(0.0), inc=k_vev)
            assert k_qmm.n == 96 and k_rope.n == 80 and k_vev.n == 16 and n_qdma == 64
            if dbg and phases == 2:
                P.wait(YQ, d_q, 16 * 64)
                P.wait(YQ, k_vev, 16)
                P.wait(YQ, k_rope, 80)
                P.wait(YQ, k_qmm, 96)
                dma(YQ, dbg_q.ap()[:, :], q_scr.ap()[:, :], d_out)
                dma(YQ, dbg_kT.ap()[:, :], kT[:, :], d_out)
                dma(YQ, dbg_va.ap()[:, :], vaug[:, :], d_out)
                dma(YQ, dbg_aug2.ap()[:, :], aug2[:, :], d_out)
                P.wait(YQ, d_out, d_out.n)
            P.flush(block)
            stkA.close()

        if phases <= 2:
            st.close()
            return nc

        with contextlib.ExitStack() as stk:
            qblk = sb(stk, "qblk", [128, 2 * 512], BF16)
            masks_sb = sb(stk, "masks_sb", [128, 4 * 512], BF16)
            pT = sb(stk, "pT", [128, 2 * 1024], BF16)
            att_nat = sb(stk, "att_nat", [128, 2 * 512], BF16)
            rc = sb(stk, "rc", [128, 4], F32)
            attst = sb(stk, "attst", [128, S], BF16)
            x_own_sb = sb(stk, "x_own_sb", [128, NKT * 1024], BF16)
            wW_sb = sb(stk, "wW_sb", [128, 2 * NKT * 128], BF16)
            g_tile = sb(stk, "g_tile", [128, 2 * 512], BF16)
            Acat_sb = sb(stk, "Acat_sb", [128, NKT * 128], BF16)
            Bcat_sb = sb(stk, "Bcat_sb", [128, D], BF16)
            lwr_sb = sb(stk, "lwr_sb", [128, 1], F32)
            sc_ps0 = pst(stk, "sc_ps0", [128, 1024], F32)
            sc_ps1 = pst(stk, "sc_ps1", [128, 1024], F32)
            avA = pst(stk, "avA", [128, 512], F32)
            avB = pst(stk, "avB", [128, 512], F32)
            trp = pst(stk, "trp", [128, 512], BF16)
            ffn_ps = pst(stk, "ffn_ps", [128, 512], F32)
            sc_ps = (sc_ps0, sc_ps1)
            for qn in (GQ, YQ):
                P.wait(qn, k_qmm, 96)
                P.wait(qn, k_qev, 96)
                P.wait(qn, k_rope, 80)
                P.wait(qn, k_trv, 64)
                P.wait(qn, k_vev, 16)
            dma(YQ, masks_sb[:, :], masks, d_misc)
            for i in range(4):
                dma(GQ, x_own_sb[:, :].rearrange("p (kt n) -> p kt n", n=1024)[:, i * 8:(i + 1) * 8, :],
                    tiled(x_own, slice(None), i * 8, (i + 1) * 8, 1024), d_xo)
            for i in range(4):
                dma(GQ, Acat_sb[:, :].rearrange("p (kt n) -> p kt n", n=128)[:, i * 8:(i + 1) * 8, :],
                    tiled(AcatT, slice(None), i * 8, (i + 1) * 8, 128), d_ac)
            dma(GQ, Bcat_sb[:, :], Bcat, d_bc)
            dma(GQ, lwr_sb[:, :], lwr, d_fb)
            d_fb_lwr = d_fb.n
            dma(GQ, ffn_auglhs[1:2, :], fbias, d_fb)
            d_fb_all = d_fb.n
            P.wait(VQ, d_bc, 16)
            P.wait(VQ, d_fb, d_fb_lwr)
            vop(VQ, "tensor_scalar", Bcat_sb[:, :], Bcat_sb[:, :], lwr_sb[:, 0:1], 2.0,
                ALU.mult, ALU.mult, inc=k_hsev)

            P.wait(TQ, k_qev, 96)
            P.wait(TQ, k_rbev, 16)
            P.wait(TQ, k_vev, 16)
            P.wait(TQ, k_rope, 80)

            ffn_items = []
            if p2_ffn:
                def it_wcA():
                    P.wait(TQ, d_ac, 64)
                    for kt in range(NKT):
                        mm(ffn_ps[0:1, 384:512], ones_col_b[:, :],
                           Acat_sb[:, kt * 128:(kt + 1) * 128],
                           kt == 0, kt == NKT - 1, inc=(k_wcA if kt == NKT - 1 else None))
                    P.wait(VQ, k_wcA, 1)
                    P.wait(VQ, k_init, 1)
                    vop(VQ, "tensor_copy", ha_auglhs[0:1, :], ffn_ps[0:1, 384:512],
                        inc=k_haev)
                ffn_items.append(it_wcA)

                def it_h(fch):
                    P.wait(TQ, d_xo, 64)
                    P.wait(TQ, k_haev, 1)
                    P.wait(TQ, d_st, d_st_aug2o)
                    P.wait(TQ, k_hsev, 1 + fch)
                    for kt in range(NKT):
                        mm(ffn_ps[:, :], Acat_sb[:, kt * 128:(kt + 1) * 128],
                           x_own_sb[:, kt * 1024 + fch * 512:kt * 1024 + (fch + 1) * 512],
                           kt == 0, False)
                    mm(ffn_ps[:, :], ha_auglhs[:, :], aug2o[:, fch * 512:(fch + 1) * 512],
                       False, True, inc=k_hmm)
                    P.wait(VQ, k_hmm, fch + 1)
                    vop(VQ, "tensor_mul", hsT[:, fch * 512:(fch + 1) * 512], ffn_ps[:, :],
                        rb_own[:, fch * 512:(fch + 1) * 512], inc=k_hsev)
                ffn_items.append(lambda: it_h(0))
                ffn_items.append(lambda: it_h(1))

                def it_load(ot):
                    ws = wW_sb[:, (ot % 2) * NKT * 128:(ot % 2 + 1) * NKT * 128]
                    P.wait(GQ, k_ffn, 2 * (ot - 2) + 2)
                    for i in range(4):
                        dma(GQ, ws.rearrange("p (kt n) -> p kt n", n=128)[:, i * 8:(i + 1) * 8, :],
                            tiled(WT, slice(ot * 128, (ot + 1) * 128), i * 8, (i + 1) * 8, 128),
                            d_wW)

                def it_wc(ot):
                    ws = wW_sb[:, (ot % 2) * NKT * 128:(ot % 2 + 1) * NKT * 128]
                    it_load(ot + 2) if ot + 2 < NKT else None
                    P.wait(TQ, d_wW, 64 * (ot + 1))
                    P.wait(TQ, k_hsev, 3)
                    P.wait(TQ, k_gev, 2 * ot)
                    P.wait(TQ, k_wcev, ot)
                    for kt in range(NKT):
                        mm(ffn_ps[0:1, 384:512], ones_col_b[:, :], ws[:, kt * 128:(kt + 1) * 128],
                           kt == 0, kt == NKT - 1, inc=(k_wcf if kt == NKT - 1 else None))
                    P.wait(VQ, k_wcf, ot + 1)
                    vop(VQ, "tensor_copy", ffn_auglhs[0:1, ot * 128:(ot + 1) * 128],
                        ffn_ps[0:1, 384:512], inc=k_wcev)

                def it_g(ot, fch):
                    ws = wW_sb[:, (ot % 2) * NKT * 128:(ot % 2 + 1) * NKT * 128]
                    e_ = 2 * ot + fch
                    P.wait(TQ, k_wcev, ot + 1)
                    P.wait(TQ, d_bc, 16)
                    P.wait(TQ, d_fb, d_fb_all)
                    P.wait(TQ, k_gev, e_)
                    for kt in range(NKT):
                        mm(ffn_ps[:, :], ws[:, kt * 128:(kt + 1) * 128],
                           x_own_sb[:, kt * 1024 + fch * 512:kt * 1024 + (fch + 1) * 512],
                           kt == 0, False)
                    mm(ffn_ps[:, :], ffn_auglhs[:, ot * 128:(ot + 1) * 128],
                       aug2o[:, fch * 512:(fch + 1) * 512], False, False)
                    mm(ffn_ps[:, :], Bcat_sb[:, ot * 128:(ot + 1) * 128],
                       hsT[:, fch * 512:(fch + 1) * 512], False, True, inc=k_ffn)
                    P.wait(VQ, k_ffn, e_ + 1)
                    P.wait(VQ, d_gw, 16 * (e_ - 1))
                    vop(VQ, "tensor_mul", g_tile[:, (e_ % 2) * 512:(e_ % 2 + 1) * 512],
                        ffn_ps[:, :], rb_own[:, fch * 512:(fch + 1) * 512], inc=k_gev)
                    P.wait(YQ, k_gev, e_ + 1)
                    dma(YQ, g_scr.ap()[ot * 128:(ot + 1) * 128, fch * 512:(fch + 1) * 512],
                        g_tile[:, (e_ % 2) * 512:(e_ % 2 + 1) * 512], d_gw)

                it_load(0)
                it_load(1)
                for ot in range(NKT):
                    ffn_items.append(lambda ot=ot: it_wc(ot))
                    ffn_items.append(lambda ot=ot: it_g(ot, 0))
                    ffn_items.append(lambda ot=ot: it_g(ot, 1))
            n_items = len(ffn_items)
            items_done = 0

            G = 0
            uend = [0, 0]
            pending_tail = None
            for u in range(p2_units):
                h, qc = u // NCH, u % NCH
                if True:
                    nkt = 4 * (qc + 1)
                    ngr = nkt // 2
                    qb = qblk[:, (u % 2) * 512:(u % 2 + 1) * 512]
                    P.wait(YQ, d_q, 16 * 64)
                    if u >= 2:
                        P.wait(YQ, k_sc, uend[u % 2])
                    dma(YQ, qb, q_scr.ap()[h * 128:(h + 1) * 128, qc * 512:(qc + 1) * 512], d_qb)
                    P.wait(TQ, d_qb, 16 * (u + 1))

                    mskcnt = {}

                    def emit_sc(gi, G_):
                        par = G_ % 2
                        sp = sc_ps[par]
                        P.wait(TQ, k_exp, G_ - 1)
                        for j in (0, 1):
                            kt = 2 * gi + j
                            mm(sp[:, j * 512:(j + 1) * 512], kT[:, kt * 128:(kt + 1) * 128],
                               qb, True, True, inc=(k_sc if j == 1 else None))
                        P.wait(SQ, k_sc, G_ + 1)
                        P.wait(SQ, k_avg, G_ - 1)
                        act(SQ, pT[:, par * 1024:(par + 1) * 1024], sp[:, :], AF.Exp,
                            f(SM_SCALE), f(SM_SHIFT), inc=k_exp)
                        if gi >= 2 * qc:
                            P.wait(VQ, d_misc, 48)
                            P.wait(VQ, k_exp, G_ + 1)
                            for j in (0, 1):
                                m = 2 * gi + j - 4 * qc
                                sl = pT[:, par * 1024 + j * 512:par * 1024 + (j + 1) * 512]
                                vop(VQ, "tensor_mul", sl, sl, masks_sb[:, m * 512:(m + 1) * 512],
                                    inc=(k_msk if j == 1 else None))
                            mskcnt[G_] = k_msk.n
                        else:
                            mskcnt[G_] = None

                    def emit_av(gi, G_):
                        par = G_ % 2
                        if mskcnt[G_] is not None:
                            P.wait(TQ, k_msk, mskcnt[G_])
                        else:
                            P.wait(TQ, k_exp, G_ + 1)
                        if u >= 1 and gi == 0:
                            P.wait(TQ, k_div, u)
                        for j in (0, 1):
                            kt = 2 * gi + j
                            for sub in range(4):
                                tgt = avA if sub < 2 else avB
                                col = 129 * (sub % 2)
                                mm(tgt[:, col:col + 129],
                                   pT[:, par * 1024 + j * 512 + sub * 128:par * 1024 + j * 512 + (sub + 1) * 128],
                                   vaug[:, 130 * kt:130 * kt + 129],
                                   kt == 0 and sub % 2 == 0, kt == nkt - 1,
                                   inc=(k_avg if (j == 1 and sub == 3) else None))

                    Gbase = G
                    for gi in range(ngr):
                        emit_sc(gi, Gbase + gi)
                        if gi == 1 and pending_tail is not None:
                            pending_tail()
                            pending_tail = None
                        if gi >= 1:
                            emit_av(gi - 1, Gbase + gi - 1)
                    emit_av(ngr - 1, Gbase + ngr - 1)
                    G = Gbase + ngr
                    uend[u % 2] = G
                    P.wait(VQ, k_avg, G)
                    P.wait(VQ, k_div, u)
                    for s_ in range(4):
                        tgt = avA if s_ < 2 else avB
                        col = 129 * (s_ % 2)
                        vop(VQ, "reciprocal", rc[:, s_:s_ + 1], tgt[:, col + 128:col + 129],
                            inc=(k_rc if s_ == 3 else None))
                    P.wait(SQ, k_rc, u + 1)
                    anat = att_nat[:, (u % 2) * 512:(u % 2 + 1) * 512]
                    P.wait(SQ, k_trm, 4 * (u - 1))
                    for s_ in range(4):
                        tgt = avA if s_ < 2 else avB
                        col = 129 * (s_ % 2)
                        act(SQ, anat[:, s_ * 128:(s_ + 1) * 128], tgt[:, col:col + 128],
                            AF.Identity, rc[:, s_:s_ + 1], f(0.0),
                            inc=(k_div if s_ == 3 else None))

                    def tail_tr(u=u, qc=qc, h=h, anat=anat):
                        P.wait(TQ, k_div, u + 1)
                        P.wait(TQ, k_trev, u)
                        for s_ in range(4):
                            tr(trp[:, s_ * 128:(s_ + 1) * 128],
                               anat[:, s_ * 128:(s_ + 1) * 128], inc=k_trm)
                        P.wait(VQ, k_trm, 4 * (u + 1))
                        P.wait(VQ, d_att, 16 * h)
                        vop(VQ, "tensor_copy", attst[:, qc * 512:(qc + 1) * 512], trp[:, :],
                            inc=k_trev)
                    pending_tail = tail_tr
                want_items = G * n_items // 1088 if p2_units == 64 else 0
                while items_done < want_items:
                    ffn_items[items_done]()
                    items_done += 1
                if qc == NCH - 1:
                    if pending_tail is not None:
                        pending_tail()
                        pending_tail = None
                    P.wait(SQ, k_trev, 16 * (h + 1))
                    dma(SQ, ag_in.ap()[h * 128:(h + 1) * 128, :], attst[:, :], d_att)
            while items_done < n_items and p2_units == 64:
                ffn_items[items_done]()
                items_done += 1
            assert p2_units < 64 or (G == 1088 and k_avg.n == 1088)
            if p2_ag:
                P.wait(GQ, d_att, 64)
                if p2_ffn:
                    P.wait(GQ, d_wW, d_wW.n)
                    P.wait(GQ, d_gw, 16 * 64)
                P.wait(GQ, d_qb, 16 * 64)
                P.wait(GQ, d_xo, 64)
                P.wait(GQ, d_ac, 64)
                P.wait(GQ, d_bc, 16)
                P.wait(GQ, d_fb, d_fb_all)
                c_ag2.n += 1
                P.emit(GQ, lambda e: e.collective_compute(
                    "AllGather", ALU.bypass, replica_groups=[list(range(8))],
                    ins=[ag_in.ap().opt()], outs=[ag_out.ap().opt()]).then_inc(c_ag2.h, 1))

            assert (not p2_ffn) or (k_ffn.n == 64 and items_done == n_items)
            if dbg and phases == 3:
                P.wait(YQ, k_trev, p2_units)
                dma(YQ, dbg_kT.ap()[0:128, 0:S], attst[:, :], d_out)
                if p2_units == 64:
                    P.wait(YQ, d_att, 16 * 4)
                    dma(YQ, dbg_agin.ap()[:, :], ag_in.ap()[:, :], d_out)
                if p2_ag:
                    P.wait(YQ, c_ag2, 1)
                    dma(YQ, dbg_agout.ap()[:, :], ag_out.ap()[:, :], d_out)
                if p2_ffn:
                    P.wait(YQ, d_gw, 16 * 64)
                    P.wait(YQ, k_hsev, 3)
                    P.wait(YQ, k_wcev, 32)
                    dma(YQ, dbg_g.ap()[:, :], g_scr.ap()[:, :], d_out)
                    dma(YQ, dbg_hs.ap()[:, :], hsT[:, :], d_out)
                    dma(YQ, dbg_aug2.ap()[:, 0:D], ffn_auglhs[:, :], d_out)
                P.wait(YQ, d_out, d_out.n)
            P.flush(block)

        if phases <= 3:
            st.close()
            return nc

        with contextlib.ExitStack() as stk:
            attg = sb(stk, "attg", [128, NKT * 1024], BF16)
            op_auglhs = sb(stk, "op_auglhs", [1, D], BF16)
            wo_sb = sb(stk, "wo_sb", [128, 2 * NKT * 128], BF16)
            g2 = sb(stk, "g2", [128, 2 * 512], BF16)
            silu_sb = sb(stk, "silu_sb", [128, 2 * 512], F32)
            xr = sb(stk, "xr", [128, 2 * 512], F32)
            tmp_ep = sb(stk, "tmp_ep", [128, 512], F32)
            out_sb = sb(stk, "out_sb", [128, 2 * 512], F32)
            op_ps0 = pst(stk, "op_ps0", [128, 512], F32)
            op_ps1 = pst(stk, "op_ps1", [128, 512], F32)
            op_ps = (op_ps0, op_ps1)
            for qn in (GQ, YQ):
                P.wait(qn, k_gev, 64)
                P.wait(qn, d_gw, 16 * 64)
                P.wait(qn, k_exp, 1088)
                P.wait(qn, k_trev, 64)
                P.wait(qn, d_att, 16 * 4)
                P.wait(qn, c_ag2, 1)
            dma(GQ, op_auglhs[0:1, :], b_o, d_fb)
            d_fb_bo = d_fb.n
            P.wait(GQ, c_ag2, 1)
            _cache = {}

            def load_attg(e, _cache=_cache):
                if "pid" not in _cache:
                    _cache["pid"] = nc.partition_id([ET.Pool])
                pid = _cache["pid"]
                for i in range(4):
                    src = ag_out.ap()[i * 1024:(i + 1) * 1024, :].rearrange(
                        "(kt p) q -> p kt q", p=128)[:, :, bass.ts(pid, OWN)]
                    dst = attg[:, :].rearrange("p (kt n) -> p kt n", n=OWN)[:, i * 8:(i + 1) * 8, :]
                    e.dma_start(out=dst, in_=src).then_inc(d_agl.h, 16)
            d_agl.n += 64
            P.emit(GQ, load_attg)

            P.wait(TQ, k_exp, 1088)
            P.wait(TQ, k_msk, k_msk.n)
            P.wait(TQ, k_gev, 64)
            for ot in range(NKT):
                par = ot % 2
                ws = wo_sb[:, par * NKT * 128:(par + 1) * NKT * 128]
                P.wait(GQ, k_op, 2 * (ot - 2) + 2)
                for i in range(4):
                    dma(GQ, ws.rearrange("p (kt n) -> p kt n", n=128)[:, i * 8:(i + 1) * 8, :],
                        tiled(woT, slice(ot * 128, (ot + 1) * 128), i * 8, (i + 1) * 8, 128), d_wo)
                P.wait(TQ, d_agl, 64)
                P.wait(TQ, d_wo, 64 * (ot + 1))
                P.wait(TQ, d_fb, d_fb_bo)
                for fch in (0, 1):
                    e_ = 2 * ot + fch
                    par2 = e_ % 2
                    P.wait(YQ, d_gw, 16 * 64)
                    P.wait(YQ, k_silu, e_ - 1)
                    dma(YQ, g2[:, par2 * 512:(par2 + 1) * 512],
                        g_scr.ap()[ot * 128:(ot + 1) * 128, fch * 512:(fch + 1) * 512], d_g)
                    P.wait(YQ, k_ep, e_ - 1)
                    dma(YQ, xr[:, par2 * 512:(par2 + 1) * 512],
                        x_own[ot * 128:(ot + 1) * 128, fch * 512:(fch + 1) * 512], d_xr)
                    P.wait(TQ, k_ep, e_ - 1)
                    for kt in range(NKT):
                        mm(op_ps[par2][:, :], ws[:, kt * 128:(kt + 1) * 128],
                           attg[:, kt * 1024 + fch * 512:kt * 1024 + (fch + 1) * 512],
                           kt == 0, False)
                    mm(op_ps[par2][:, :], op_auglhs[0:1, ot * 128:(ot + 1) * 128],
                       ones_row_b[0:1, :], False, True, inc=k_op)
                    P.wait(SQ, d_g, 16 * (e_ + 1))
                    P.wait(SQ, k_ep, e_ - 1)
                    act(SQ, silu_sb[:, par2 * 512:(par2 + 1) * 512],
                        g2[:, par2 * 512:(par2 + 1) * 512], AF.Silu, f(1.0), f(0.0),
                        inc=k_silu)
                    P.wait(VQ, k_op, e_ + 1)
                    P.wait(VQ, k_silu, e_ + 1)
                    P.wait(VQ, d_xr, 16 * (e_ + 1))
                    P.wait(VQ, d_out, 16 * (e_ - 1))
                    vop(VQ, "tensor_mul", tmp_ep[:, :], op_ps[par2][:, :],
                        silu_sb[:, par2 * 512:(par2 + 1) * 512], inc=k_v)
                    P.wait(VQ, k_v, k_v.n)
                    vop(VQ, "tensor_add", out_sb[:, par2 * 512:(par2 + 1) * 512],
                        tmp_ep[:, :], xr[:, par2 * 512:(par2 + 1) * 512], inc=k_ep)
                    P.wait(YQ, k_ep, e_ + 1)
                    dma(YQ, outT[ot * 128:(ot + 1) * 128, fch * 512:(fch + 1) * 512],
                        out_sb[:, par2 * 512:(par2 + 1) * 512], d_out)
            assert k_op.n == 64 and d_out.n == 16 * 64
            P.wait(YQ, d_out, 16 * 64)
            P.flush(block)

    st.close()
    return nc


def _host_stage(inputs):
    x = np.asarray(inputs["L0_attn_x"], dtype=np.float32).reshape(S, D)
    w_qkv = np.asarray(inputs["L0_attn_w_qkv"], dtype=np.float32)
    b_qkv = np.asarray(inputs["L0_attn_b_qkv"], dtype=np.float32)
    w_o = np.asarray(inputs["L0_attn_w_o"], dtype=np.float32)
    b_o = np.asarray(inputs["L0_attn_b_o"], dtype=np.float32)
    lora_A = np.asarray(inputs["L0_ffn_lora_A"], dtype=np.float32)
    lora_B = np.asarray(inputs["L0_ffn_lora_B"], dtype=np.float32)
    W = np.asarray(inputs["L0_ffn_W"], dtype=np.float32)
    fbias = np.asarray(inputs["L0_ffn_bias"], dtype=np.float32)
    lw = np.asarray(inputs["L0_ffn_lora_weights"], dtype=np.float32)

    xT = np.ascontiguousarray(x.T)
    woT = np.ascontiguousarray(w_o.T)
    WT = np.ascontiguousarray(W.T)
    AcatT = np.ascontiguousarray(lora_A.reshape(128, D).T)
    Bcat = np.ascontiguousarray(lora_B.transpose(0, 2, 1).reshape(128, D))
    lwr = np.ascontiguousarray(np.repeat(lw, 16).reshape(128, 1))

    inv_freq = (1.0 / 10000.0 ** (np.arange(0, HD, 2, dtype=np.float64) / HD))
    ang = inv_freq[:, None] * np.arange(S, dtype=np.float64)[None, :]
    cos64 = np.cos(ang).astype(np.float32)
    sin64 = np.sin(ang).astype(np.float32)
    cosT = np.ascontiguousarray(np.vstack([cos64, cos64]))
    sinT = np.ascontiguousarray(np.vstack([-sin64, sin64]))
    swap = np.zeros((128, 128), dtype=np.float32)
    swap[(np.arange(128) + 64) % 128, np.arange(128)] = 1.0
    swap = swap.astype(ml_dtypes.bfloat16)

    kk = np.arange(128)[:, None]
    qq = np.arange(512)[None, :]
    mk = np.concatenate([( (m * 128 + kk) <= qq).astype(np.float32) for m in range(4)],
                        axis=1)
    masks = mk.astype(ml_dtypes.bfloat16)
    ident = np.eye(128, dtype=ml_dtypes.bfloat16)

    def even_odd(base):
        return [base + 2 * j for j in range(64)] + [base + 2 * j + 1 for j in range(64)]

    in_maps = []
    for c in range(8):
        rows = []
        for hh in range(4 * c, 4 * c + 4):
            rows += even_odd(128 * hh)
        rows += even_odd(4096 + 128 * c)
        rows += list(range(5120 + 128 * c, 5120 + 128 * (c + 1)))
        rows = np.asarray(rows)
        wqkvT = np.ascontiguousarray(w_qkv[rows, :].T)
        bq = np.ascontiguousarray(b_qkv[rows].reshape(1, QKVO))
        in_maps.append({
            "xT": xT,
            "x_own": np.ascontiguousarray(xT[:, c * OWN:(c + 1) * OWN]),
            "wqkvT": wqkvT,
            "bqkv": bq,
            "woT": woT,
            "WT": WT,
            "AcatT": AcatT,
            "Bcat": Bcat,
            "b_o": np.ascontiguousarray(b_o.reshape(1, D)),
            "fbias": np.ascontiguousarray(fbias.reshape(1, D)),
            "lwr": lwr,
            "cosT": cosT,
            "sinT": sinT,
            "masks": masks,
            "ident": ident,
            "swapm": swap,
        })
    return in_maps


def kernel(**inputs):
    in_maps = _host_stage(inputs)
    nc = build_nc()
    res = run_bass_kernel_spmd(nc, in_maps, core_ids=list(range(8)))
    cols = [res.results[c]["outT"] for c in range(8)]
    full = np.concatenate(cols, axis=1)
    return np.ascontiguousarray(full.T).reshape(1, S, D).astype(np.float32)
